# version 26
# baseline (speedup 1.0000x reference)
import contextlib
import numpy as np
import concourse.bass as bass
import concourse.mybir as mybir
from concourse.bass_utils import run_bass_kernel_spmd

F32 = mybir.dt.float32
BF16 = mybir.dt.bfloat16
AX = mybir.AxisListType.X
ALU = mybir.AluOpType
AF = mybir.ActivationFunctionType

D = 2048
T = 4096
NH = 8
MYT = 2048
H = 8192
QSCALE = float(128) ** -0.5
MAGIC = float(np.float32(1.5 * 2 ** 23))
EPS = 1e-5

ENGINES = ("tensor", "vector", "scalar", "gpsimd", "sync")

DEBUG_TAPS = []
STOP_AFTER = None


class _Op:
    __slots__ = ("eng", "fn", "dma", "need_inc", "incval", "waits", "tag")

    def __init__(self, eng, fn, dma, tag=""):
        self.eng, self.fn, self.dma, self.tag = eng, fn, dma, tag
        self.need_inc = False
        self.incval = None
        self.waits = []


class Prog:

    def __init__(self, nc, stack):
        self.nc = nc
        self.stack = stack
        self.phase = 0
        self.streams = {e: [] for e in ENGINES}
        self.writer = {}
        self.readers = {}
        self.sem_handles = {}
        self.n_ops = 0
        self.dma_seq = {e: 0 for e in ENGINES}
        self.DMA_DEPTH = 8

    def op(self, eng, fn, reads=(), writes=(), dma=False, tag=""):
        o = _Op(eng, fn, dma, tag)
        if dma:
            o.need_inc = True
            seq = self.dma_seq[eng]
            self.dma_seq[eng] = seq + 1
            slot = seq % self.DMA_DEPTH
            use = seq // self.DMA_DEPTH + 1
            o.incval = (f"d{eng}{slot}", 16 * use)
            if use > 1:
                o.waits.append(("sem", f"d{eng}{slot}", 16 * (use - 1)))
        self.streams[eng].append(o)
        self.n_ops += 1
        deps = []
        for b in reads:
            deps.extend(self.writer.get(b, ()))
            rkey = id(o) if dma else (eng, False)
            self.readers.setdefault(b, {})[rkey] = o
        for b in writes:
            ws = self.writer.get(b, [])
            deps.extend(ws)
            rs = self.readers.get(b, {})
            deps.extend(rs.values())
            if rs:
                self.writer[b] = [o]
            else:
                ws = list(ws)
                ws.append(o)
                self.writer[b] = ws
            self.readers[b] = {}
        seen = set()
        for dep in deps:
            if dep is o or id(dep) in seen:
                continue
            seen.add(id(dep))
            if dep.eng == eng == "tensor" and not dep.dma and not dma:
                continue
            if dep.dma:
                o.waits.append(("sem", dep.incval[0], dep.incval[1]))
            else:
                dep.need_inc = True
                o.waits.append(("op", dep))
        return o

    def _sem(self, name):
        if name not in self.sem_handles:
            self.sem_handles[name] = self.stack.enter_context(self.nc.semaphore(name))
        return self.sem_handles[name]

    def flush(self):
        ph = self.phase
        self.phase += 1
        streams, self.streams = self.streams, {e: [] for e in ENGINES}
        self.writer = {}
        self.readers = {}

        counters = {}
        for eng in ENGINES:
            for o in streams[eng]:
                if o.need_inc and not o.dma:
                    name = f"s{ph}c{eng}"
                    counters[name] = counters.get(name, 0) + 1
                    o.incval = (name, counters[name])
        dma_final = {}
        for eng in ENGINES:
            specs = []
            n = self.dma_seq[eng]
            for slot in range(self.DMA_DEPTH):
                uses = (n - slot + self.DMA_DEPTH - 1) // self.DMA_DEPTH
                if uses > 0:
                    specs.append((f"d{eng}{slot}", 16 * uses))
                    self._sem(f"d{eng}{slot}")
            if specs:
                dma_final[eng] = specs
        for name in counters:
            self._sem(name)
        bsem = self._sem(f"bar{ph}")

        def wait_specs(o):
            for w in o.waits:
                if w[0] == "sem":
                    yield (w[1], w[2])
                else:
                    dep = w[1]
                    if dep.incval is not None:
                        yield dep.incval

        with self.nc.Block() as blk:
            for eng in ENGINES:
                def run(e, _eng=eng, _ops=streams[eng]):
                    waited = {}
                    for o in _ops:
                        for name, v in wait_specs(o):
                            if v > waited.get(name, 0):
                                e.wait_ge(self._sem(name), v)
                                waited[name] = v
                        r = o.fn(e)
                        if o.need_inc:
                            r.then_inc(self._sem(o.incval[0]), 16 if o.dma else 1)
                    for name, v in dma_final.get(_eng, ()):
                        if v > waited.get(name, 0):
                            e.wait_ge(self._sem(name), v)
                            waited[name] = v
                    e.nop().then_inc(bsem, 1)
                    e.wait_ge(bsem, 5)
                getattr(blk, eng)(run)


def ts(i, n=128):
    return slice(i * n, (i + 1) * n)


def build_nc():
    nc = bass.Bass("TRN2", target_bir_lowering=False)
    stack = contextlib.ExitStack()
    sb = stack.enter_context

    x = nc.dram_tensor("x", [T, D], F32, kind="ExternalInput").ap()
    wqkv = nc.dram_tensor("wqkv", [D, 3 * 1024], F32, kind="ExternalInput").ap()
    wout = nc.dram_tensor("wout", [1024, D], F32, kind="ExternalInput").ap()
    xres = nc.dram_tensor("xres", [MYT, D], F32, kind="ExternalInput").ap()
    bout = nc.dram_tensor("bout", [D], F32, kind="ExternalInput").ap()
    wg = nc.dram_tensor("wg", [D, H], F32, kind="ExternalInput").ap()
    wv_ = nc.dram_tensor("wv", [D, H], F32, kind="ExternalInput").ap()
    wo2 = nc.dram_tensor("wo2", [H, D], F32, kind="ExternalInput").ap()
    bg = nc.dram_tensor("bg", [H], F32, kind="ExternalInput").ap()
    bv = nc.dram_tensor("bv", [H], F32, kind="ExternalInput").ap()
    bo2 = nc.dram_tensor("bo2", [D], F32, kind="ExternalInput").ap()
    out = nc.dram_tensor("out", [MYT, D], F32, kind="ExternalOutput").ap()

    xb_s = nc.dram_tensor("xb_s", [T, D], BF16, kind="Internal").ap()
    qk_s = nc.dram_tensor("qk_s", [2048, T], BF16, kind="Internal").ap()
    v_s = nc.dram_tensor("v_s", [NH, 128, T], BF16, kind="Internal").ap()
    partial_s = nc.dram_tensor("partial_s", [2, 2, MYT // 2, D], F32,
                               kind="Internal").ap()
    rs_out = nc.dram_tensor("rs_out", [MYT, D], F32, kind="Internal").ap()
    y_s = nc.dram_tensor("y_s", [MYT, D], F32, kind="Internal").ap()
    nx_s = nc.dram_tensor("nx_s", [MYT, D], BF16, kind="Internal").ap()
    h_s = nc.dram_tensor("h_s", [64, 128, MYT], F32, kind="Internal").ap()
    wgt_s = nc.dram_tensor("wgt_s", [D, H], BF16, kind="Internal").ap()
    wvt_s = nc.dram_tensor("wvt_s", [D, H], BF16, kind="Internal").ap()
    wo2t_s = nc.dram_tensor("wo2t_s", [H, D], BF16, kind="Internal").ap()
    rb = nc.dram_tensor("row_bounce", [4096], F32, kind="Internal").ap()
    wqkvb_s = nc.dram_tensor("wqkvb_s", [2048, 2048], BF16, kind="Internal").ap()

    taps = dict(xb_s=xb_s, qk_s=qk_s, v_s=v_s, partial_s=partial_s,
                rs_out=rs_out, y_s=y_s, nx_s=nx_s, h_s=h_s)

    p = Prog(nc, stack)
    O = p.op

    ident = sb(nc.sbuf_tensor("ident", [128, 128], F32)).ap()
    ones_col_b = sb(nc.sbuf_tensor("ones_col_b", [128, 1], BF16)).ap()
    ones_col_f = sb(nc.sbuf_tensor("ones_col_f", [128, 1], F32)).ap()
    ones_row_b = sb(nc.sbuf_tensor("ones_row_b", [1, 128], BF16)).ap()
    ones_row_f = sb(nc.sbuf_tensor("ones_row_f", [1, 128], F32)).ap()
    bg_cols = sb(nc.sbuf_tensor("bg_cols", [128, 64], F32)).ap()
    bv_cols = sb(nc.sbuf_tensor("bv_cols", [128, 64], F32)).ap()
    gam = sb(nc.sbuf_tensor("gam", [128, 12], F32)).ap()
    gx_cols = sb(nc.sbuf_tensor("gx_cols", [128, 16], F32)).ap()
    sh_cols = sb(nc.sbuf_tensor("sh_cols", [128, 16], F32)).ap()
    ghn_cols = sb(nc.sbuf_tensor("ghn_cols", [128, 16], F32)).ap()
    p01 = contextlib.ExitStack()
    stack.callback(p01.close)
    tps = p01.enter_context(nc.psum_tensor("tps", [128, 512], F32)).ap()
    taps["gam"] = gam

    O("gpsimd", lambda g: g.memset(ident[:], 1.0), writes=["ident"])
    O("gpsimd", lambda g: g.affine_select(
        ident[:], ident[:], [[1, 128]], ALU.is_equal, 0.0, channel_multiplier=-1),
        reads=["ident"], writes=["ident"])
    O("gpsimd", lambda g: g.memset(ones_col_b[:], 1.0), writes=["ocb"])
    O("gpsimd", lambda g: g.memset(ones_col_f[:], 1.0), writes=["ocf"])
    O("gpsimd", lambda g: g.memset(ones_row_b[:], 1.0), writes=["orb"])
    O("gpsimd", lambda g: g.memset(ones_row_f[:], 1.0), writes=["orf"])
    O("gpsimd", lambda g: g.memset(gam[:], 0.0), writes=["gam"])
    for i in range(4):
        O("gpsimd", lambda g, _i=i: g.dma_start(
            xb_s[ts(_i, 1024), :], x[ts(_i, 1024), :]),
            writes=[f"xb_s{i}"], dma=True)
        O("gpsimd", lambda g, _i=i: g.dma_start(
            wqkvb_s[:, ts(_i, 512)], wqkv[:, ts(_i, 512)]),
            writes=[f"wqkvb{i}"], dma=True, tag="wqk_cast")
    if STOP_AFTER == "P0":
        _maybe_taps(nc, p, taps, "P0")
        p.flush()
        return nc, stack

    with contextlib.ExitStack() as ph:
        pb = ph.enter_context
        xfm = [pb(nc.sbuf_tensor(f"xfm{i}", [128, 16 * 1024], BF16)).ap() for i in range(2)]
        wcol4 = [pb(nc.sbuf_tensor(f"wcol4_{i}", [128, 16 * 512], BF16)).ap()
                 for i in range(3)]
        wv_sb = pb(nc.sbuf_tensor("wv_sb", [128, 16 * 1024], BF16)).ap()
        qkstg = [pb(nc.sbuf_tensor(f"qkstg{i}", [128, 1024], BF16)).ap() for i in range(2)]
        vstg = [pb(nc.sbuf_tensor(f"vstg{i}", [128, 1024], BF16)).ap() for i in range(4)]
        pp_st = [pb(nc.sbuf_tensor(f"pp_st{i}", [128, 2048], F32)).ap() for i in range(2)]
        pp_acc = pb(nc.sbuf_tensor("pp_acc", [128, 2], F32)).ap()
        gsum = pb(nc.sbuf_tensor("gsum", [128, 3], F32)).ap()
        ps = [pb(nc.psum_tensor(f"p1_ps{i}", [128, 512], F32)).ap() for i in range(4)]
        qk_ps = ps
        v_ps = ps

        O("gpsimd", lambda g: g.memset(gsum[:], 0.0), writes=["gsum"])
        for e in range(16):
            O("gpsimd", lambda g, _e=e: g.dma_start(
                wv_sb[:, ts(_e, 1024)], wqkv[ts(_e), 2048:3072]),
                writes=["wv_sb"], dma=True)

        pp_jobs = []
        for mi, w in enumerate((wg, wv_)):
            for r_ in range(16):
                for cc in range(4):
                    pp_jobs.append((mi, w[ts(r_), ts(cc, 2048)]))
        for r_ in range(64):
            pp_jobs.append((2, wo2[ts(r_), :]))
        pp_n = len(pp_jobs)
        pp_i = [0]

        def pp_step():
            if pp_i[0] >= pp_n:
                return
            i = pp_i[0]
            pp_i[0] += 1
            mi, sl = pp_jobs[i]
            b = i % 2
            O(("sync", "gpsimd")[i % 2],
              lambda s, _sl=sl, _b=b: s.dma_start(pp_st[_b][:], _sl),
              reads=[f"ppr{b}"], writes=[f"pps{b}"], dma=True, tag="pp_load")
            O("vector", lambda v, _b=b: v.reduce_sum(
                pp_acc[:, _b:_b + 1], pp_st[_b][:], AX, apply_absolute_value=True),
                reads=[f"pps{b}"], writes=[f"ppa{b}", f"ppr{b}"], tag="pp_abs")
            O("vector", lambda v, _b=b, _m=mi: v.tensor_tensor(
                gsum[:, _m:_m + 1], gsum[:, _m:_m + 1], pp_acc[:, _b:_b + 1], ALU.add),
                reads=[f"ppa{b}", "gsum"], writes=["gsum"], tag="pp_add")

        def load_xfm(g):
            for e in range(16):
                O("sync", lambda s, _g=g, _e=e: s.dma_start(
                    xfm[_g % 2][:, ts(_e, 1024)], xb_s[ts(_g, 1024), ts(_e)],
                    transpose=True),
                    reads=[f"xb_s{g}"], writes=[f"xfm{g % 2}"], dma=True, tag="xfm")

        def load_wc4(j):
            f4 = j % 4
            O("gpsimd", lambda gg, _f4=f4, _wb=wcol4[j % 3]: gg.dma_start(
                _wb[:].rearrange("p (et f) -> p et f", et=16),
                wqkvb_s[:, ts(_f4, 512)].rearrange("(et p) f -> p et f", p=128)),
                reads=[f"wqkvb{f4}"], writes=[f"wc4_{j % 3}"], dma=True, tag="wcol")

        slot = [0]

        def pp_slot():
            n = 3 if slot[0] < 48 else 2
            slot[0] += 1
            for _ in range(n):
                pp_step()

        load_xfm(0)
        load_wc4(0)
        load_wc4(1)
        for g in range(4):
            if g + 1 < 4:
                load_xfm(g + 1)
            xf = xfm[g % 2]
            xk = f"xfm{g % 2}"
            for f in range(16):
                j = g * 4 + f // 4
                wb = wcol4[j % 3]
                wk = f"wc4_{j % 3}"
                if f % 4 == 0 and j + 2 < 16:
                    load_wc4(j + 2)
                for half in range(2):
                    bank = qk_ps[2 * (f % 2) + half]
                    bk = f"p1ps{2 * (f % 2) + half}"
                    for e in range(16):
                        O("tensor", lambda t, _e=e, _f=f, _h=half, _b=bank, _xf=xf, _wb=wb: t.matmul(
                            _b[:], _wb[:, _e * 512 + (_f % 4) * 128: _e * 512 + (_f % 4 + 1) * 128],
                            _xf[:, _e * 1024 + _h * 512: _e * 1024 + (_h + 1) * 512],
                            start=(_e == 0), stop=(_e == 15)),
                            reads=[wk, xk], writes=[bk], tag="qk_mm")
                    st = qkstg[f % 2]
                    sk = f"qkstg{f % 2}"
                    scale = QSCALE if f < 8 else 1.0
                    O("scalar", lambda s, _b=bank, _st=st, _h=half, _sc=scale: s.activation(
                        _st[:, _h * 512:(_h + 1) * 512], _b[:], AF.Copy,
                        bias=0.0, scale=_sc),
                        reads=[bk], writes=[sk], tag="qk_ev")
                O("sync", lambda s, _f=f, _g=g, _st=qkstg[f % 2]: s.dma_start(
                    qk_s[ts(_f), ts(_g, 1024)], _st[:]),
                    reads=[sk], writes=["qk_s"], dma=True, tag="qk_st")
                pp_slot()
            for half in range(2):
                for sub in range(8):
                    bank = v_ps[2 * (sub % 2) + half]
                    bk = f"p1ps{2 * (sub % 2) + half}"
                    for e in range(16):
                        O("tensor", lambda t, _s=sub, _e=e, _h=half, _b=bank, _xf=xf: t.matmul(
                            _b[:], _xf[:, _e * 1024 + _s * 128: _e * 1024 + (_s + 1) * 128],
                            wv_sb[:, _e * 1024 + _h * 512: _e * 1024 + (_h + 1) * 512],
                            start=(_e == 0), stop=(_e == 15)),
                            reads=["wv_sb", xk], writes=[bk], tag="v_mm")
                    for j in range(4):
                        O("scalar", lambda s, _b=bank, _s2=sub, _j=j: s.activation(
                            vstg[_j][:, _s2 * 128:(_s2 + 1) * 128],
                            _b[:, _j * 128:(_j + 1) * 128],
                            AF.Copy, bias=0.0, scale=1.0),
                            reads=[bk], writes=[f"vstg{j}"], tag="v_ev")
                    if sub % 2 == 1:
                        pp_slot()
                for j in range(4):
                    hd = half * 4 + j
                    O("scalar", lambda s, _g=g, _hd=hd, _j=j: s.dma_start(
                        v_s[_hd, :, ts(_g, 1024)], vstg[_j][:]),
                        reads=[f"vstg{j}"], writes=["v_s"], dma=True, tag="v_st")

        while pp_i[0] < pp_n:
            pp_step()

        for mi in range(3):
            O("tensor", lambda t, _m=mi: t.matmul(
                tps[0:1, _m:_m + 1], ones_col_f[:], gsum[:, _m:_m + 1],
                start=True, stop=True),
                reads=["gsum", "ocf"], writes=["tps"], tag="g_mm")
        O("vector", lambda v: v.tensor_scalar(
            gam[0:1, 9:12], tps[0:1, 0:3], 1.0 / float(D * H), 1e-5, ALU.mult, ALU.max),
            reads=["tps"], writes=["gamrow"], tag="g_sc")
        for mi in range(3):
            O("tensor", lambda t, _m=mi: t.matmul(
                tps[:, 384 + _m:385 + _m], ones_row_f[:], gam[0:1, 9 + _m:10 + _m],
                start=True, stop=True),
                reads=["gamrow", "orf"], writes=["tps"], tag="g_bc")
            O("vector", lambda v, _m=mi: v.tensor_scalar(
                gam[:, 3 * _m:3 * _m + 1], tps[:, 384 + _m:385 + _m], 0.5, None, ALU.mult),
                reads=["tps"], writes=["gam"], tag="g_thr")
            O("vector", lambda v, _m=mi: v.tensor_scalar(
                gam[:, 3 * _m + 1:3 * _m + 2], tps[:, 384 + _m:385 + _m], -0.5, None,
                ALU.mult), reads=["tps"], writes=["gam"], tag="g_nthr")
            O("vector", lambda v, _m=mi: v.tensor_copy(
                gam[:, 3 * _m + 2:3 * _m + 3], tps[:, 384 + _m:385 + _m]),
                reads=["tps"], writes=["gam"], tag="g_gw")

        _maybe_taps(nc, p, taps, "P1")
        p.flush()
        if STOP_AFTER == "P1":
            return nc, stack

    p01.close()
    with contextlib.ExitStack() as p23:
        ofm = p23.enter_context(
            nc.sbuf_tensor("ofm", [128, NH * T], BF16)).ap()
        wout_sb = p23.enter_context(
            nc.sbuf_tensor("wout_sb", [128, 8 * 2048], BF16)).ap()
        tw_st = [p23.enter_context(nc.sbuf_tensor(f"tw_st{i}", [128, 2048], F32)).ap()
                 for i in range(2)]
        tw_t1 = p23.enter_context(nc.sbuf_tensor("tw_t1", [128, 2048], F32)).ap()
        tw_o = [p23.enter_context(nc.sbuf_tensor(f"tw_o{i}", [128, 2048], BF16)).ap()
                for i in range(2)]
        with contextlib.ExitStack() as ph:
            pb = ph.enter_context
            qh = [pb(nc.sbuf_tensor(f"qh{i}", [128, T], BF16)).ap() for i in range(2)]
            kh = [pb(nc.sbuf_tensor(f"kh{i}", [128, T], BF16)).ap() for i in range(2)]
            vh = [pb(nc.sbuf_tensor(f"vh{i}", [128, T], BF16)).ap() for i in range(2)]
            pt = [pb(nc.sbuf_tensor(f"pt{i}", [128, 512], BF16)).ap() for i in range(3)]
            masks = pb(nc.sbuf_tensor("masks", [128, 4 * 512], F32)).ap()
            q2big = pb(nc.sbuf_tensor("q2big", [128, T], BF16)).ap()
            nsq = pb(nc.sbuf_tensor("nsq", [1, 24], F32)).ap()
            negmcol = pb(nc.sbuf_tensor("negmcol", [128, 1], F32)).ap()
            linv = pb(nc.sbuf_tensor("linv", [1, 512], F32)).ap()
            bcb = pb(nc.sbuf_tensor("bcb", [128, 512], F32)).ap()
            s_ps = [pb(nc.psum_tensor(f"s_ps{i}", [128, 512], F32)).ap() for i in range(3)]
            o_ps = [pb(nc.psum_tensor(f"o_ps{i}", [128, 512], F32)).ap() for i in range(2)]
            l_ps = [pb(nc.psum_tensor(f"l_ps{i}", [128, 512], F32)).ap() for i in range(2)]
            bc_ps = pb(nc.psum_tensor("bc_ps", [128, 512], F32)).ap()

            tern_jobs = []
            for mi_, (w_, c_) in enumerate(((wg, wgt_s), (wv_, wvt_s), (wo2, wo2t_s))):
                rows = D if mi_ < 2 else H
                for r_ in range(rows // 128):
                    for cc_ in range(w_.shape[1] // 2048):
                        tern_jobs.append((mi_, w_[ts(r_), ts(cc_, 2048)],
                                          c_[ts(r_), ts(cc_, 2048)]))
            tern_i = [0]

            def tern_step(st_eng="gpsimd"):
                if tern_i[0] >= len(tern_jobs):
                    return
                i = tern_i[0]
                tern_i[0] += 1
                mi_, wsl, csl = tern_jobs[i]
                bb_ = i % 2
                O("sync", lambda s, _w=wsl, _b=bb_: s.dma_start(tw_st[_b][:], _w),
                    reads=[f"twr{bb_}"], writes=[f"tws{bb_}"], dma=True, tag="tw_ld")
                O("vector", lambda v, _b=bb_, _m=mi_: v.tensor_scalar(
                    tw_t1[:], tw_st[_b][:], gam[:, 3 * _m:3 * _m + 1], None,
                    ALU.is_gt),
                    reads=[f"tws{bb_}", "gam"], writes=["twt1"], tag="tw_gt")
                O("vector", lambda v, _b=bb_, _m=mi_: v.scalar_tensor_tensor(
                    tw_o[_b][:], tw_st[_b][:], gam[:, 3 * _m + 1:3 * _m + 2],
                    tw_t1[:], ALU.is_lt, ALU.subtract),
                    reads=[f"tws{bb_}", "twt1", "gam"],
                    writes=[f"two{bb_}", f"twr{bb_}"], tag="tw_tern")
                O(st_eng, lambda g, _csl=csl, _b=bb_: g.dma_start(_csl, tw_o[_b][:]),
                    reads=[f"two{bb_}"], writes=["terncache"], dma=True, tag="tw_st")

            O("gpsimd", lambda g: g.memset(masks[:], 0.0), writes=["masks"])
            for r_ in range(4):
                O("gpsimd", lambda g, _r=r_: g.affine_select(
                    masks[:, ts(_r, 512)], masks[:, ts(_r, 512)], [[1, 512]],
                    ALU.is_ge, -1e30, base=-128 * _r, channel_multiplier=-1),
                    reads=["masks"], writes=["masks"])

            def load_head(h):
                b = h % 2
                O("sync", lambda s, _h=h, _b=b: s.dma_start(qh[_b][:], qk_s[ts(_h), :]),
                  reads=["qk_s"], writes=[f"qh{b}"], dma=True, tag="ld_q")
                O("sync", lambda s, _h=h, _b=b: s.dma_start(
                    kh[_b][:], qk_s[1024 + _h * 128:1024 + (_h + 1) * 128, :]),
                    reads=["qk_s"], writes=[f"kh{b}"], dma=True, tag="ld_k")
                O("sync", lambda s, _h=h, _b=b: s.dma_start(vh[_b][:], v_s[_h, :, :]),
                  reads=["v_s"], writes=[f"vh{b}"], dma=True, tag="ld_v")

            load_head(0)
            for h in range(NH):
                if h + 1 < NH:
                    load_head(h + 1)
                b = h % 2
                qk_, kk_, vk_ = f"qh{b}", f"kh{b}", f"vh{b}"
                for side, buf, bk_ in ((0, qh[b], qk_), (1, kh[b], kk_)):
                    O("scalar", lambda sc, _buf=buf: sc.activation(
                        q2big[:], _buf[:], AF.Square, bias=0.0, scale=1.0),
                        reads=[bk_], writes=["q2big"], tag="cs_sq")
                    for c in range(8):
                        O("tensor", lambda t, _c=c: t.matmul(
                            s_ps[_c % 3][0:1, :], ones_col_b[:], q2big[:, ts(_c, 512)],
                            start=True, stop=True),
                            reads=["q2big", "ocb"], writes=[f"s_ps{c % 3}"], tag="cs_mm")
                        O("vector", lambda v, _side=side, _c=c: v.reduce_max(
                            nsq[0:1, 8 * _side + _c:8 * _side + _c + 1],
                            s_ps[_c % 3][0:1, :], AX),
                            reads=[f"s_ps{c % 3}"], writes=["nsq"], tag="cs_red")
                O("vector", lambda v: v.reduce_max(nsq[0:1, 16:17], nsq[0:1, 0:8], AX),
                  reads=["nsq"], writes=["nsq"], tag="cs_mq")
                O("vector", lambda v: v.reduce_max(nsq[0:1, 17:18], nsq[0:1, 8:16], AX),
                  reads=["nsq"], writes=["nsq"], tag="cs_mk")
                O("vector", lambda v: v.tensor_tensor(
                    nsq[0:1, 18:19], nsq[0:1, 16:17], nsq[0:1, 17:18], ALU.mult),
                    reads=["nsq"], writes=["nsq"], tag="cs_prod")
                O("scalar", lambda sc: sc.activation(
                    nsq[0:1, 19:20], nsq[0:1, 18:19], AF.Ln, bias=0.0, scale=1.0),
                    reads=["nsq"], writes=["nsq2"], tag="cs_ln")
                O("scalar", lambda sc: sc.activation(
                    nsq[0:1, 20:21], nsq[0:1, 19:20], AF.Exp, bias=0.0, scale=0.5),
                    reads=["nsq2"], writes=["nsq3"], tag="cs_exp")
                O("vector", lambda v: v.tensor_scalar(
                    nsq[0:1, 21:22], nsq[0:1, 20:21], -1.0, None, ALU.mult),
                    reads=["nsq3"], writes=["nsq4"], tag="cs_neg")
                O("tensor", lambda t: t.matmul(
                    bc_ps[:, 0:1], ones_row_f[:], nsq[0:1, 21:22], start=True, stop=True),
                    reads=["nsq4", "orf"], writes=["bc_ps"], tag="cs_bc")
                O("vector", lambda v: v.tensor_copy(negmcol[:], bc_ps[:, 0:1]),
                  reads=["bc_ps"], writes=["negmcol"], tag="cs_col")

                steps = [(g, kt) for g in range(8) for kt in range(4 * g + 4)]
                ns = len(steps)

                def emit_S(i):
                    g, kt = steps[i]
                    sb_ = i % 3
                    O("tensor", lambda t, _kt=kt, _g=g, _s=sb_, _kh=kh[b], _qh=qh[b]: t.matmul(
                        s_ps[_s][:], _kh[:, ts(_kt)], _qh[:, ts(_g, 512)],
                        start=True, stop=True),
                        reads=[kk_, qk_], writes=[f"s_ps{sb_}"], tag="st_mm")
                    if kt >= 4 * g:
                        r = kt - 4 * g
                        O("vector", lambda v, _s=sb_, _r=r: v.tensor_tensor(
                            s_ps[_s][:], s_ps[_s][:], masks[:, ts(_r, 512)], ALU.add),
                            reads=[f"s_ps{sb_}", "masks"], writes=[f"s_ps{sb_}"],
                            tag="st_mask")

                def emit_exp(i):
                    sb_ = i % 3
                    O("scalar", lambda sc, _s=sb_: sc.activation(
                        pt[_s][:], s_ps[_s][:], AF.Exp, bias=negmcol[:], scale=1.0),
                        reads=[f"s_ps{sb_}", "negmcol"], writes=[f"pt{sb_}"],
                        tag="st_exp")

                def emit_AVl(i):
                    g, kt = steps[i]
                    sb_ = i % 3
                    ob = g % 2
                    nk = 4 * g + 4
                    O("tensor", lambda t, _kt=kt, _ob=ob, _s=sb_, _vh=vh[b], _nk=nk: t.matmul(
                        o_ps[_ob][:], _vh[:, ts(_kt)], pt[_s][:],
                        start=(_kt == 0), stop=(_kt == _nk - 1)),
                        reads=[vk_, f"pt{sb_}"], writes=[f"o_ps{ob}"], tag="av_mm")
                    O("tensor", lambda t, _kt=kt, _ob=ob, _s=sb_, _nk=nk: t.matmul(
                        l_ps[_ob][0:1, :], ones_col_b[:], pt[_s][:],
                        start=(_kt == 0), stop=(_kt == _nk - 1)),
                        reads=["ocb", f"pt{sb_}"], writes=[f"l_ps{ob}"], tag="l_mm")

                def emit_fin(g):
                    ob = g % 2
                    O("vector", lambda v, _ob=ob: v.reciprocal(linv[:], l_ps[_ob][0:1, :]),
                      reads=[f"l_ps{ob}"], writes=["linv"], tag="o_linv")
                    O("tensor", lambda t: t.matmul(
                        bc_ps[:], ones_row_f[:], linv[:], start=True, stop=True),
                        reads=["linv", "orf"], writes=["bc_ps"], tag="o_bc")
                    O("vector", lambda v: v.tensor_copy(bcb[:], bc_ps[:]),
                      reads=["bc_ps"], writes=["bcb"], tag="o_bcc")
                    O("vector", lambda v, _ob=ob, _h=h, _g=g: v.tensor_tensor(
                        ofm[:, _h * T + _g * 512:_h * T + (_g + 1) * 512],
                        o_ps[_ob][:], bcb[:], ALU.mult),
                        reads=[f"o_ps{ob}", "bcb"], writes=["ofm"], tag="o_mul")

                emit_S(0)
                if ns > 1:
                    emit_S(1)
                pending = []
                for i in range(ns):
                    emit_exp(i)
                    if i + 2 < ns:
                        emit_S(i + 2)
                    emit_AVl(i)
                    while pending and pending[0][0] <= i - 2:
                        emit_fin(pending.pop(0)[1])
                        if h >= 1:
                            tern_step()
                            tern_step()
                            tern_step()
                    g, kt = steps[i]
                    if kt == 4 * g + 3:
                        pending.append((i, g))
                while pending:
                    emit_fin(pending.pop(0)[1])
                if 2 <= h <= 5:
                    for i_ in range(2):
                        et = (h - 2) * 2 + i_
                        O("gpsimd", lambda g_, _et=et: g_.dma_start(
                            wout_sb[:, ts(_et, 2048)], wout[ts(_et), :]),
                            writes=["wout_sb"], dma=True, tag="ld_wout")

            _maybe_taps(nc, p, taps, "P2")
            p.flush()
            if STOP_AFTER == "P2":
                return nc, stack

        with contextlib.ExitStack() as ph:
            pb = ph.enter_context
            pst = [pb(nc.sbuf_tensor(f"pst{i}", [128, 512], F32)).ap() for i in range(3)]
            pp_ps = [pb(nc.psum_tensor(f"pp_ps{i}", [128, 512], F32)).ap() for i in range(6)]
            brow2 = pb(nc.sbuf_tensor("brow2", [1, H], F32)).ap()
            brov2 = pb(nc.sbuf_tensor("brov2", [1, H], F32)).ap()
            bps = pb(nc.psum_tensor("bps", [128, 16], F32)).ap()
            O("scalar", lambda s: s.dma_start(
                brow2[:], bg.rearrange("(a b) -> a b", a=1)),
                writes=["brow2"], dma=True)
            O("scalar", lambda s: s.dma_start(
                brov2[:], bv.rearrange("(a b) -> a b", a=1)),
                writes=["brov2"], dma=True)

            def bias_batch(bt):
                src, sk_, dstc, dk_ = ((brow2, "brow2", bg_cols, "bgc") if bt < 8
                                       else (brov2, "brov2", bv_cols, "bvc"))
                hb_ = bt % 2
                for q in range(8):
                    j = (bt % 8) * 8 + q
                    O("tensor", lambda t, _j=j, _h=hb_, _q=q, _s=src: t.matmul(
                        bps[:, _h * 8 + _q:_h * 8 + _q + 1], _s[:, ts(_j)],
                        ones_row_f[:, 0:1], start=True, stop=True),
                        reads=[sk_, "orf"], writes=[f"bps{hb_}"], tag="b_mm")
                O("vector", lambda v, _bt=bt, _h=hb_, _d=dstc: v.tensor_copy(
                    _d[:, (_bt % 8) * 8:(_bt % 8) * 8 + 8], bps[:, _h * 8:_h * 8 + 8]),
                    reads=[f"bps{hb_}"], writes=[dk_], tag="b_cp")

            ccs1 = p.stack.enter_context(nc.semaphore("ccsem1"))
            ccs2 = p.stack.enter_context(nc.semaphore("ccsem2"))

            def _cc_issue(g, j, sem):
                g.collective_compute(
                    "ReduceScatter", ALU.add,
                    replica_groups=[[0, 1], [2, 3], [4, 5], [6, 7]],
                    ins=[partial_s[j].rearrange("a b c -> (a b) c")],
                    outs=[rs_out[j * 1024:(j + 1) * 1024, :]],
                ).then_inc(sem)

            tt_order = list(range(0, 8)) + list(range(16, 24))                 + list(range(8, 16)) + list(range(24, 32))
            for ti, tt in enumerate(tt_order):
                half = 0 if (tt % 16) < 8 else 1
                tern_step()
                tern_step()
                if ti % 2 == 0:
                    bias_batch(ti // 2)
                if ti == 16:
                    O("gpsimd", lambda g: _cc_issue(g, 0, ccs1),
                      reads=["part0"], tag="rs1_issue")
                for ec in range(4):
                    bi = (tt * 4 + ec) % 6
                    bank = pp_ps[bi]
                    bk = f"pp_ps{bi}"
                    for et in range(8):
                        O("tensor", lambda t, _tt=tt, _ec=ec, _et=et, _b=bank: t.matmul(
                            _b[:], ofm[:, _et * T + _tt * 128:_et * T + (_tt + 1) * 128],
                            wout_sb[:, _et * 2048 + _ec * 512:_et * 2048 + (_ec + 1) * 512],
                            start=(_et == 0), stop=(_et == 7)),
                            reads=["ofm", "wout_sb"], writes=[bk], tag="op_mm")
                    sti = (tt * 4 + ec) % 3
                    O("scalar", lambda s, _b=bank, _st=pst[sti]: s.activation(
                        _st[:], _b[:], AF.Copy, bias=0.0, scale=1.0),
                        reads=[bk], writes=[f"pst{sti}"], tag="op_ev")
                    O("sync", lambda s, _tt=tt, _ec=ec, _h=half, _st=pst[sti]: s.dma_start(
                        partial_s[_h, _tt // 16, ts(_tt % 8), ts(_ec, 512)], _st[:]),
                        reads=[f"pst{sti}"], writes=[f"part{half}"], dma=True,
                        tag="op_st")
            O("gpsimd", lambda g: _cc_issue(g, 1, ccs2),
              reads=["part1"], tag="rs2_issue")
            while tern_i[0] < len(tern_jobs):
                tern_step(st_eng="scalar")
            O("gpsimd", lambda g: g.wait_ge(ccs1, 1), writes=["rs0"],
              tag="rs1_wait")
            _maybe_taps(nc, p, taps, "P3a")
            p.flush()
            if STOP_AFTER == "P3a":
                return nc, stack


    with contextlib.ExitStack() as p45:
        runmax = p45.enter_context(nc.sbuf_tensor("runmax", [128, MYT], F32)).ap()
        runmin = p45.enter_context(nc.sbuf_tensor("runmin", [128, MYT], F32)).ap()

        p34 = contextlib.ExitStack()
        p45.callback(p34.close)
        nx_fm = p34.enter_context(nc.sbuf_tensor("nx_fm", [128, 16 * 2048], BF16)).ap()
        sgbc = p34.enter_context(nc.sbuf_tensor("sgbc", [128, MYT], F32)).ap()
        svbc = sgbc

        with contextlib.ExitStack() as ph:
            pb = ph.enter_context
            rsld = [pb(nc.sbuf_tensor(f"rsld{i}", [128, D], F32)).ap() for i in range(2)]
            xr = [pb(nc.sbuf_tensor(f"xr{i}", [128, D], F32)).ap() for i in range(2)]
            bbrow = pb(nc.sbuf_tensor("bbrow", [1, D], F32)).ap()
            bbbc = pb(nc.sbuf_tensor("bbbc", [128, D], F32)).ap()
            yt = pb(nc.sbuf_tensor("yt", [128, D], F32)).ap()
            sq = pb(nc.sbuf_tensor("sq", [128, D], F32)).ap()
            nxt = pb(nc.sbuf_tensor("nxt", [128, D], BF16)).ap()
            stat = pb(nc.sbuf_tensor("stat", [128, 16], F32)).ap()
            srow = pb(nc.sbuf_tensor("srow", [1, 512], F32)).ap()
            tps = pb(nc.psum_tensor("tps3", [128, 512], F32)).ap()

            O("gpsimd", lambda g: g.wait_ge(ccs2, 1), writes=["rs1"],
              tag="rs2_wait")
            O("sync", lambda s: s.dma_start(bbrow[:], bout.rearrange("(a b) -> a b", a=1)),
              writes=["bbrow"], dma=True, tag="bb_ld")
            for c in range(4):
                O("tensor", lambda t, _c=c: t.matmul(
                    tps[:, 0:512], ones_row_f[:], bbrow[:, ts(_c, 512)],
                    start=True, stop=True),
                    reads=["bbrow", "orf"], writes=["tps"], tag="bb_mm")
                O("vector", lambda v, _c=c: v.tensor_copy(bbbc[:, ts(_c, 512)],
                                                          tps[:, 0:512]),
                  reads=["tps"], writes=["bbbc"], tag="bb_cp")

            for tt in range(16):
                b = tt % 2
                O("sync", lambda s, _tt=tt, _b=b: s.dma_start(
                    rsld[_b][:], rs_out[ts(_tt), :]),
                    reads=[f"rs{0 if tt < 8 else 1}"],
                    writes=[f"rsld{b}"], dma=True, tag="rs_ld")
                O("sync", lambda s, _tt=tt, _b=b: s.dma_start(
                    xr[_b][:], xres[ts(_tt), :]),
                    writes=[f"xr{b}"], dma=True, tag="xr_ld")
                O("vector", lambda v, _b=b: v.tensor_tensor(
                    rsld[_b][:], rsld[_b][:], xr[_b][:], ALU.add),
                    reads=[f"rsld{b}", f"xr{b}"], writes=[f"rsld{b}"], tag="res_add")
                O("vector", lambda v, _b=b: v.tensor_tensor(
                    rsld[_b][:], rsld[_b][:], bbbc[:], ALU.add),
                    reads=[f"rsld{b}", "bbbc"], writes=[f"rsld{b}"], tag="bias_add")
                O("scalar", lambda s, _b=b: s.activation(
                    sq[:], rsld[_b][:], AF.Square, bias=0.0, scale=1.0,
                    accum_out=stat[:, 0:1]),
                    reads=[f"rsld{b}"], writes=["sq", "st0"], tag="ln_sq")
                O("vector", lambda v, _b=b: v.reduce_sum(stat[:, 1:2], rsld[_b][:], AX),
                  reads=[f"rsld{b}"], writes=["st1"], tag="ln_sum")
                O("vector", lambda v: v.tensor_scalar(
                    stat[:, 2:3], stat[:, 1:2], 1.0 / D, None, ALU.mult),
                    reads=["st1"], writes=["st2"], tag="ln_m")
                O("vector", lambda v: v.tensor_scalar(
                    stat[:, 3:4], stat[:, 0:1], 1.0 / D, None, ALU.mult),
                    reads=["st0"], writes=["st3"], tag="ln_ex2")
                O("scalar", lambda s: s.activation(
                    stat[:, 4:5], stat[:, 2:3], AF.Square, bias=0.0, scale=1.0),
                    reads=["st2"], writes=["st4"], tag="ln_m2")
                O("vector", lambda v: v.tensor_tensor(
                    stat[:, 5:6], stat[:, 3:4], stat[:, 4:5], ALU.subtract),
                    reads=["st3", "st4"], writes=["st5"], tag="ln_var")
                O("vector", lambda v: v.tensor_scalar(
                    stat[:, 14:15], stat[:, 5:6], EPS, None, ALU.add),
                    reads=["st5"], writes=["st14"], tag="ln_eps")
                O("scalar", lambda s: s.activation(
                    stat[:, 6:7], stat[:, 14:15], AF.Ln, bias=0.0, scale=1.0),
                    reads=["st14"], writes=["st6"], tag="ln_ln")
                O("scalar", lambda s: s.activation(
                    stat[:, 7:8], stat[:, 6:7], AF.Exp, bias=0.0, scale=-0.5),
                    reads=["st6"], writes=["st7"], tag="ln_rstd")
                O("vector", lambda v: v.tensor_tensor(
                    stat[:, 8:9], stat[:, 2:3], stat[:, 7:8], ALU.mult),
                    reads=["st2", "st7"], writes=["st8"], tag="ln_mr")
                O("vector", lambda v: v.tensor_scalar(
                    stat[:, 9:10], stat[:, 8:9], -1.0, None, ALU.mult),
                    reads=["st8"], writes=["st9"], tag="ln_nmr")
                O("vector", lambda v, _b=b: v.tensor_scalar(
                    yt[:], rsld[_b][:], stat[:, 7:8], stat[:, 9:10],
                    ALU.mult, ALU.add),
                    reads=[f"rsld{b}", "st9", "st7"], writes=["yt"], tag="ln_y")
                O("sync", lambda s, _tt=tt: s.dma_start(y_s[ts(_tt), :], yt[:]),
                  reads=["yt"], writes=["y_s"], dma=True, tag="y_st")
                O("vector", lambda v: v.reduce_max(
                    stat[:, 10:11], yt[:], AX, apply_absolute_value=True),
                    reads=["yt"], writes=["st10"], tag="qx_amax")
                O("vector", lambda v: v.tensor_scalar(
                    stat[:, 11:12], stat[:, 10:11], 1e-5, None, ALU.max),
                    reads=["st10"], writes=["st11"], tag="qx_clip")
                O("vector", lambda v: v.reciprocal(stat[:, 12:13], stat[:, 11:12]),
                  reads=["st11"], writes=["st12"], tag="qx_rec")
                O("vector", lambda v: v.tensor_scalar(
                    stat[:, 13:14], stat[:, 12:13], -127.0, None, ALU.mult),
                    reads=["st12"], writes=["st13"], tag="qx_nsc")
                O("vector", lambda v, _tt=tt: v.tensor_scalar(
                    gx_cols[:, _tt:_tt + 1], stat[:, 11:12], 1.0 / 127.0, None, ALU.mult),
                    reads=["st11"], writes=["gx_cols"], tag="qx_gx")
                O("scalar", lambda s, _b=b: s.activation(
                    sq[:], yt[:], AF.Copy, bias=0.0, scale=stat[:, 13:14]),
                    reads=["yt", "st13", "sq"], writes=["sq"], tag="qx_scale")
                O("vector", lambda v: v.tensor_scalar(
                    sq[:], sq[:], MAGIC, -MAGIC, ALU.add, ALU.add),
                    reads=["sq"], writes=["sq"], tag="qx_round")
                O("vector", lambda v: v.tensor_copy(nxt[:], sq[:]),
                    reads=["sq"], writes=["nxt"], tag="qx_cast")
                O("sync", lambda s, _tt=tt: s.dma_start(nx_s[ts(_tt), :], nxt[:]),
                  reads=["nxt"], writes=["nx_s"], dma=True, tag="nx_st")

            for cch in range(4):
                for j in range(4):
                    tt = cch * 4 + j
                    O("tensor", lambda t, _tt=tt, _j=j: t.matmul(
                        tps[0:1, ts(_j)], gx_cols[:, _tt:_tt + 1], ident[:],
                        start=True, stop=True),
                        reads=["gx_cols", "ident"], writes=["tps"], tag="sg_tr")
                O("vector", lambda v: v.tensor_copy(srow[:], tps[0:1, 0:512]),
                  reads=["tps"], writes=["srow"], tag="sg_row")
                O("tensor", lambda t: t.matmul(
                    tps[:, 0:512], ones_row_f[:], srow[:], start=True, stop=True),
                    reads=["srow", "orf"], writes=["tps"], tag="sg_bc")
                O("vector", lambda v, _c=cch: v.tensor_copy(
                    sgbc[:, ts(_c, 512)], tps[:, 0:512]),
                    reads=["tps"], writes=["sgbc"], tag="sg_cp")
            for et in range(16):
                O("sync", lambda s, _et=et: s.dma_start(
                    nx_fm[:, ts(_et, 2048)], nx_s[:, ts(_et)], transpose=True),
                    reads=["nx_s"], writes=["nx_fm"], dma=True, tag="nx_tr")
            _maybe_taps(nc, p, taps, "P3")
            p.flush()
            if STOP_AFTER == "P3":
                return nc, stack
        with contextlib.ExitStack() as ph:
            pb = ph.enter_context
            wgt = [pb(nc.sbuf_tensor(f"wgt{i}", [128, 2048], BF16)).ap()
                   for i in range(4)]
            g1 = pb(nc.sbuf_tensor("g1", [128, 512], F32)).ap()
            gate = pb(nc.sbuf_tensor("gate", [128, MYT], F32)).ap()
            v1 = pb(nc.sbuf_tensor("v1", [128, 512], F32)).ap()
            htile = [pb(nc.sbuf_tensor(f"htile{i}", [128, 512], F32)).ap()
                     for i in range(2)]
            f_ps = [pb(nc.psum_tensor(f"f_ps{i}", [128, 512], F32)).ap()
                    for i in range(4)]

            O("gpsimd", lambda g: g.memset(runmax[:], 0.0), writes=["runmax"])
            O("gpsimd", lambda g: g.memset(runmin[:], 0.0), writes=["runmin"])

            def tern_load(mat, ht, buf):
                src = wgt_s if mat == 0 else wvt_s
                O("sync", lambda s, _b=buf, _ht=ht, _w=src: s.dma_start(
                    wgt[_b][:].rearrange("p (et f) -> p et f", et=16),
                    _w[:, ts(_ht)].rearrange("(et p) f -> p et f", p=128)),
                    writes=[f"wgt{buf}"], dma=True, tag="w_ld")

            tern_load(0, 0, 0)
            tern_load(1, 0, 2)
            for ht in range(64):
                gb, vb = ht % 2, 2 + ht % 2
                if ht + 1 < 64:
                    tern_load(0, ht + 1, (ht + 1) % 2)
                    tern_load(1, ht + 1, 2 + (ht + 1) % 2)
                for mat, wb_, psoff in ((0, gb, 0), (1, vb, 2)):
                    for tc in range(4):
                        bank = f_ps[psoff + tc % 2]
                        bk = f"f_ps{psoff + tc % 2}"
                        for et in range(16):
                            O("tensor", lambda t, _w=wgt[wb_], _et=et, _tc=tc, _b=bank: t.matmul(
                                _b[:], _w[:, ts(_et)],
                                nx_fm[:, _et * 2048 + _tc * 512:_et * 2048 + (_tc + 1) * 512],
                                start=(_et == 0), stop=(_et == 15)),
                                reads=[f"wgt{wb_}", "nx_fm"], writes=[bk], tag="f_mm")
                        if mat == 0:
                            O("vector", lambda v, _b=bank, _tc=tc: v.scalar_tensor_tensor(
                                g1[:], _b[:], gam[:, 2:3], sgbc[:, ts(_tc, 512)],
                                ALU.mult, ALU.mult),
                                reads=[bk, "gam", "sgbc"], writes=["g1"], tag="g_sc")
                            O("scalar", lambda s, _ht=ht, _tc=tc: s.activation(
                                gate[:, ts(_tc, 512)], g1[:], AF.Silu,
                                bias=bg_cols[:, _ht:_ht + 1], scale=1.0),
                                reads=["g1", "bgc"], writes=["gate"], tag="g_silu")
                        else:
                            O("vector", lambda v, _b=bank, _tc=tc: v.scalar_tensor_tensor(
                                v1[:], _b[:], gam[:, 5:6], sgbc[:, ts(_tc, 512)],
                                ALU.mult, ALU.mult),
                                reads=[bk, "gam", "sgbc"], writes=["v1"], tag="v_sc")
                            hb = tc % 2
                            O("vector", lambda v, _ht=ht, _tc=tc, _hb=hb: v.scalar_tensor_tensor(
                                htile[_hb][:], v1[:], bv_cols[:, _ht:_ht + 1],
                                gate[:, ts(_tc, 512)], ALU.add, ALU.mult),
                                reads=["v1", "bvc", "gate"], writes=[f"htile{hb}"],
                                tag="h_mul")
                            O("vector", lambda v, _tc=tc, _hb=hb: v.tensor_tensor(
                                runmax[:, ts(_tc, 512)], runmax[:, ts(_tc, 512)],
                                htile[_hb][:], ALU.max),
                                reads=["runmax", f"htile{hb}"], writes=["runmax"],
                                tag="h_max")
                            O("vector", lambda v, _tc=tc, _hb=hb: v.tensor_tensor(
                                runmin[:, ts(_tc, 512)], runmin[:, ts(_tc, 512)],
                                htile[_hb][:], ALU.min),
                                reads=["runmin", f"htile{hb}"], writes=["runmin"],
                                tag="h_min")
                            O("scalar", lambda s, _ht=ht, _tc=tc, _hb=hb: s.dma_start(
                                h_s[_ht, :, ts(_tc, 512)], htile[_hb][:]),
                                reads=[f"htile{hb}"], writes=["h_s"], dma=True,
                                tag="h_st")
            _maybe_taps(nc, p, taps, "P4")
            p.flush()
            if STOP_AFTER == "P4":
                return nc, stack
        p34.close()

        with contextlib.ExitStack() as ph:
            pb = ph.enter_context
            ghrow = pb(nc.sbuf_tensor("ghrow", [128, 16], F32)).ap()
            nbc = pb(nc.sbuf_tensor("nbc", [128, MYT], F32)).ap()
            bo2bc = pb(nc.sbuf_tensor("bo2bc", [128, D], F32)).ap()
            srow2 = pb(nc.sbuf_tensor("srow2", [1, 512], F32)).ap()
            outacc = pb(nc.sbuf_tensor("outacc", [128, 4 * 2048], F32)).ap()
            wo2t = [pb(nc.sbuf_tensor(f"wo2t{i}", [128, 8 * 2048], BF16)).ap()
                    for i in range(2)]
            hstage = [pb(nc.sbuf_tensor(f"hstage{i}", [128, 8 * 512], F32)).ap()
                      for i in range(2)]
            nh_sb = [pb(nc.sbuf_tensor(f"nh_sb{i}", [128, 8 * 512], BF16)).ap()
                     for i in range(2)]
            ep = pb(nc.sbuf_tensor("ep", [128, D], F32)).ap()
            yb = [pb(nc.sbuf_tensor(f"yb{i}", [128, D], F32)).ap() for i in range(2)]
            stat2 = pb(nc.sbuf_tensor("stat2", [128, 16], F32)).ap()
            tps = pb(nc.psum_tensor("tps5", [128, 512], F32)).ap()
            x_ps = [pb(nc.psum_tensor(f"x_ps{i}", [128, 512], F32)).ap()
                    for i in range(5)]

            O("vector", lambda v: v.tensor_scalar(
                runmin[:], runmin[:], -1.0, None, ALU.mult),
                reads=["runmin"], writes=["runmin"], tag="gh_neg")
            O("vector", lambda v: v.tensor_tensor(
                runmax[:], runmax[:], runmin[:], ALU.max),
                reads=["runmax", "runmin"], writes=["runmax"], tag="gh_comb")
            for tt in range(16):
                O("tensor", lambda t, _tt=tt: t.transpose(
                    x_ps[_tt % 2][:, 0:128], runmax[:, ts(_tt)], ident[:]),
                    reads=["runmax", "ident"], writes=[f"x_ps{tt % 2}"], tag="gh_tr")
                O("vector", lambda v, _tt=tt: v.reduce_max(
                    ghrow[:, _tt:_tt + 1], x_ps[_tt % 2][:, 0:128], AX),
                    reads=[f"x_ps{tt % 2}"], writes=["ghrow"], tag="gh_red")
            O("vector", lambda v: v.tensor_scalar(
                ghrow[:], ghrow[:], 1e-5, None, ALU.max),
                reads=["ghrow"], writes=["ghrow"], tag="gh_clip")
            O("vector", lambda v: v.tensor_scalar(
                sh_cols[:], ghrow[:], gam[:, 8:9], 1.0 / 127.0, ALU.mult, ALU.mult),
                reads=["ghrow", "gam"], writes=["sh_cols"], tag="sh_mk")
            O("vector", lambda v: v.reciprocal(ghn_cols[:], ghrow[:]),
              reads=["ghrow"], writes=["ghn_cols"], tag="ghn_rec")
            O("vector", lambda v: v.tensor_scalar(
                ghn_cols[:], ghn_cols[:], -127.0, None, ALU.mult),
                reads=["ghn_cols"], writes=["ghn_cols"], tag="ghn_mul")
            for cch in range(4):
                for j in range(4):
                    tt = cch * 4 + j
                    O("tensor", lambda t, _tt=tt, _j=j: t.matmul(
                        tps[0:1, ts(_j)], ghn_cols[:, _tt:_tt + 1], ident[:],
                        start=True, stop=True),
                        reads=["ghn_cols", "ident"], writes=["tps"], tag="nb_tr")
                O("vector", lambda v: v.tensor_copy(srow2[:], tps[0:1, 0:512]),
                  reads=["tps"], writes=["srow2"], tag="nb_row")
                O("tensor", lambda t: t.matmul(
                    tps[:, 0:512], ones_row_f[:], srow2[:], start=True, stop=True),
                    reads=["srow2", "orf"], writes=["tps"], tag="nb_bc")
                O("vector", lambda v, _c=cch: v.tensor_copy(nbc[:, ts(_c, 512)],
                                                            tps[:, 0:512]),
                  reads=["tps"], writes=["nbc"], tag="nb_cp")
            for c in range(4):
                O("sync", lambda s, _c=c: s.dma_start(
                    srow2[:], bo2.rearrange("(a b) -> a b", a=1)[:, ts(_c, 512)]),
                    reads=["srow2r"], writes=["srow2"], dma=True, tag="bo2_ld")
                O("tensor", lambda t, _c=c: t.matmul(
                    tps[:, 0:512], ones_row_f[:], srow2[:],
                    start=True, stop=True),
                    reads=["srow2", "orf"], writes=["tps"], tag="bo2_mm")
                O("vector", lambda v, _c=c: v.tensor_copy(bo2bc[:, ts(_c, 512)],
                                                          tps[:, 0:512]),
                  reads=["tps", "srow2"], writes=["bo2bc", "srow2r"], tag="bo2_cp")

            def p5_loads_w(th, hc):
                b = hc % 2
                for hsub in range(8):
                    ht = hc * 8 + hsub
                    O("sync", lambda s, _ht=ht, _b=b, _hs=hsub: s.dma_start(
                        wo2t[_b][:, ts(_hs, 2048)], wo2t_s[ts(_ht), :]),
                        reads=[f"wo2r{b}"], writes=[f"wo2t{b}"], dma=True,
                        tag="o_wld")

            def p5_loads_h(th, hc):
                b = hc % 2
                for hsub in range(8):
                    ht = hc * 8 + hsub
                    O("scalar", lambda s, _ht=ht, _th=th, _b=b, _hs=hsub: s.dma_start(
                        hstage[_b][:, ts(_hs, 512)],
                        h_s[_ht, :, _th * 512:(_th + 1) * 512]),
                        reads=[f"hqr{b}"], writes=[f"hstage{b}"], dma=True,
                        tag="h_ld")

            def p5_loads(th, hc):
                p5_loads_w(th, hc)
                p5_loads_h(th, hc)

            def p5_quant(th, hc):
                b = hc % 2
                for hs in range(8):
                    O("vector", lambda v, _b=b, _th=th, _hs=hs: v.tensor_tensor(
                        hstage[_b][:, ts(_hs, 512)], hstage[_b][:, ts(_hs, 512)],
                        nbc[:, _th * 512:(_th + 1) * 512], ALU.mult),
                        reads=[f"hstage{b}", "nbc"], writes=[f"hstage{b}"], tag="h_sc")
                O("vector", lambda v, _b=b: v.tensor_scalar(
                    hstage[_b][:], hstage[_b][:], MAGIC, -MAGIC, ALU.add, ALU.add),
                    reads=[f"hstage{b}"], writes=[f"hstage{b}"], tag="h_round")
                O("vector", lambda v, _b=b: v.tensor_copy(nh_sb[_b][:], hstage[_b][:]),
                    reads=[f"hstage{b}"], writes=[f"nh_sb{b}", f"hqr{b}"],
                    tag="h_cast")

            def ep_tt(th_, tt):
                gt = th_ * 4 + tt
                b = tt % 2
                O("sync", lambda s, _gt=gt, _b=b: s.dma_start(
                    yb[_b][:], y_s[ts(_gt), :]),
                    reads=[f"ybr{b}"], writes=[f"yb{b}"], dma=True, tag="y_ld")
                O("vector", lambda v, _tt=tt, _gt=gt: v.scalar_tensor_tensor(
                    ep[:], outacc[:, _tt * 2048:(_tt + 1) * 2048],
                    sh_cols[:, _gt:_gt + 1], bo2bc[:], ALU.mult, ALU.add),
                    reads=["outacc", "sh_cols", "bo2bc"], writes=["ep"], tag="ep_sc")
                O("vector", lambda v, _b=b: v.tensor_tensor(
                    ep[:], ep[:], yb[_b][:], ALU.add),
                    reads=["ep", f"yb{b}"], writes=["ep", f"ybr{b}"], tag="ep_res")
                O("scalar", lambda s, _tt=tt: s.activation(
                    outacc[:, _tt * 2048:(_tt + 1) * 2048], ep[:],
                    AF.Square, bias=0.0, scale=1.0, accum_out=stat2[:, 0:1]),
                    reads=["ep"], writes=["outacc", "s2t0"], tag="l2_sq")
                O("scalar", lambda s, _tt=tt: s.activation(
                    outacc[:, _tt * 2048:(_tt + 1) * 2048], ep[:],
                    AF.Copy, bias=0.0, scale=1.0 / D, accum_out=stat2[:, 2:3]),
                    reads=["ep"], writes=["outacc", "s2t2"], tag="l2_mean")
                O("vector", lambda v: v.tensor_scalar(
                    stat2[:, 3:4], stat2[:, 0:1], 1.0 / D, None, ALU.mult),
                    reads=["s2t0"], writes=["s2t3"], tag="l2_ex2")
                O("scalar", lambda s: s.activation(
                    stat2[:, 4:5], stat2[:, 2:3], AF.Square, bias=0.0, scale=1.0),
                    reads=["s2t2"], writes=["s2t4"], tag="l2_m2")
                O("vector", lambda v: v.tensor_scalar(
                    stat2[:, 14:15], stat2[:, 3:4], stat2[:, 4:5],
                    EPS, ALU.subtract, ALU.add),
                    reads=["s2t3", "s2t4"], writes=["s2t14"], tag="l2_var")
                O("scalar", lambda s: s.activation(
                    stat2[:, 6:7], stat2[:, 14:15], AF.Ln, bias=0.0, scale=1.0),
                    reads=["s2t14"], writes=["s2t6"], tag="l2_ln")
                O("scalar", lambda s: s.activation(
                    stat2[:, 7:8], stat2[:, 6:7], AF.Exp, bias=0.0, scale=-0.5),
                    reads=["s2t6"], writes=["s2t7"], tag="l2_rstd")
                O("vector", lambda v: v.tensor_tensor(
                    stat2[:, 8:9], stat2[:, 2:3], stat2[:, 7:8], ALU.mult),
                    reads=["s2t2", "s2t7"], writes=["s2t8"], tag="l2_mr")
                O("vector", lambda v: v.tensor_scalar(
                    stat2[:, 9:10], stat2[:, 8:9], -1.0, None, ALU.mult),
                    reads=["s2t8"], writes=["s2t9"], tag="l2_nmr")
                O("vector", lambda v, _b=b: v.tensor_scalar(
                    yb[_b][:], ep[:], stat2[:, 7:8], stat2[:, 9:10],
                    ALU.mult, ALU.add),
                    reads=["ep", "s2t9", "s2t7"], writes=[f"yb{b}"], tag="l2_y")
                O("sync", lambda s, _gt=gt, _b=b: s.dma_start(
                    out[ts(_gt), :], yb[_b][:]),
                    reads=[f"yb{b}"], writes=["out", f"ybr{b}"], dma=True,
                    tag="out_st")

            p5_loads(0, 0)
            p5_quant(0, 0)
            for th in range(4):
                for hc in range(8):
                    b = hc % 2
                    if hc + 1 < 8:
                        p5_loads(th, hc + 1)
                        p5_quant(th, hc + 1)
                        if hc == 6 and th + 1 < 4:
                            p5_loads_h(th + 1, 0)
                    elif th + 1 < 4:
                        p5_loads_w(th + 1, 0)
                    for tt in range(4):
                        if hc == 0 and th >= 1:
                            ep_tt(th - 1, tt)
                        for ec in range(4):
                            bi = (tt * 4 + ec) % 5
                            bank = x_ps[bi]
                            bk = f"x_ps{bi}"
                            for hsub in range(8):
                                O("tensor", lambda t, _tt=tt, _ec=ec, _hs=hsub, _b=bank, _bb=b: t.matmul(
                                    _b[:],
                                    nh_sb[_bb][:, _hs * 512 + _tt * 128:_hs * 512 + (_tt + 1) * 128],
                                    wo2t[_bb][:, _hs * 2048 + _ec * 512:_hs * 2048 + (_ec + 1) * 512],
                                    start=(_hs == 0), stop=(_hs == 7)),
                                    reads=[f"nh_sb{b}", f"wo2t{b}"], writes=[bk],
                                    tag="o_mm")
                            if hc == 0:
                                ceng = "scalar" if th >= 1 else "vector"
                                O(ceng, lambda v, _tt=tt, _ec=ec, _b=bank: (
                                    v.tensor_copy(
                                        outacc[:, _tt * 2048 + _ec * 512:_tt * 2048 + (_ec + 1) * 512],
                                        _b[:])
                                    if hasattr(v, "tensor_copy") else
                                    v.activation(
                                        outacc[:, _tt * 2048 + _ec * 512:_tt * 2048 + (_ec + 1) * 512],
                                        _b[:], AF.Copy, bias=0.0, scale=1.0)),
                                    reads=[bk], writes=["outacc"], tag="o_cp")
                            else:
                                O("vector", lambda v, _tt=tt, _ec=ec, _b=bank: v.tensor_tensor(
                                    outacc[:, _tt * 2048 + _ec * 512:_tt * 2048 + (_ec + 1) * 512],
                                    outacc[:, _tt * 2048 + _ec * 512:_tt * 2048 + (_ec + 1) * 512],
                                    _b[:], ALU.add),
                                    reads=[bk, "outacc"], writes=["outacc"], tag="o_add")
                    if hc == 7 and th + 1 < 4:
                        p5_quant(th + 1, 0)
                    O("tensor", lambda t: t.nop(), reads=[f"wo2t{b}", f"nh_sb{b}"],
                      writes=[f"wo2r{b}"], tag="o_fence")
            for tt in range(4):
                ep_tt(3, tt)
            _maybe_taps(nc, p, taps, None)
            p.flush()
    return nc, stack


def _maybe_taps(nc, p, taps, phase):
    if phase is not None and STOP_AFTER != phase:
        return
    for name in DEBUG_TAPS:
        apx = taps[name]
        dst = nc.dram_tensor(f"tap_{name}", list(apx.shape), apx.dtype,
                             kind="ExternalOutput").ap()
        p.op("gpsimd", lambda g, _d=dst, _s=apx: g.dma_start(_d, _s),
             reads=[name], dma=True, tag=f"tap_{name}")


def core_in_map(core, x_full, w_qkv, w_out, b_out, gate_w, gate_b, val_w,
                val_b, out_w, out_b):
    B, hh = core // 2, core % 2
    ac = np.ascontiguousarray
    wq = np.concatenate([
        w_qkv[:, 1024 * hh:1024 * (hh + 1)],
        w_qkv[:, 2048 + 1024 * hh:2048 + 1024 * (hh + 1)],
        w_qkv[:, 4096 + 1024 * hh:4096 + 1024 * (hh + 1)]], axis=1)
    return dict(
        x=ac(x_full[B]),
        xres=ac(x_full[B, MYT * hh:MYT * (hh + 1)]),
        wqkv=ac(wq),
        wout=ac(w_out[1024 * hh:1024 * (hh + 1)]),
        bout=ac(b_out),
        wg=ac(gate_w.T),
        wv=ac(val_w.T),
        wo2=ac(out_w.T),
        bg=ac(gate_b),
        bv=ac(val_b),
        bo2=ac(out_b),
    )


def kernel(L0_attn_x, L0_attn_w_qkv, L0_attn_w_out, L0_attn_b_out,
           L0_ffn_gate_w, L0_ffn_gate_b, L0_ffn_val_w, L0_ffn_val_b,
           L0_ffn_out_w, L0_ffn_out_b):
    x = np.asarray(L0_attn_x, dtype=np.float32)
    args = [np.asarray(a, dtype=np.float32) for a in (
        L0_attn_w_qkv, L0_attn_w_out, L0_attn_b_out, L0_ffn_gate_w,
        L0_ffn_gate_b, L0_ffn_val_w, L0_ffn_val_b, L0_ffn_out_w, L0_ffn_out_b)]
    in_maps = [core_in_map(c, x, *args) for c in range(8)]
    nc, stack = build_nc()
    try:
        res = run_bass_kernel_spmd(nc, in_maps, core_ids=list(range(8)))
    finally:
        stack.close()
    out = np.empty((4, T, D), dtype=np.float32)
    for c in range(8):
        B, hh = c // 2, c % 2
        out[B, MYT * hh:MYT * (hh + 1), :] = np.asarray(res.results[c]["out"])
    return out


# version 28 (final)
# speedup vs baseline: 1.0332x; 1.0332x over previous
import contextlib
import numpy as np
import concourse.bass as bass
import concourse.mybir as mybir
from concourse.bass_utils import run_bass_kernel_spmd

F32 = mybir.dt.float32
BF16 = mybir.dt.bfloat16
AX = mybir.AxisListType.X
ALU = mybir.AluOpType
AF = mybir.ActivationFunctionType

D = 2048
T = 4096
NH = 8
MYT = 2048
H = 8192
QSCALE = float(128) ** -0.5
MAGIC = float(np.float32(1.5 * 2 ** 23))
EPS = 1e-5

ENGINES = ("tensor", "vector", "scalar", "gpsimd", "sync")

DEBUG_TAPS = []
STOP_AFTER = None


class _Op:
    __slots__ = ("eng", "fn", "dma", "need_inc", "incval", "waits", "tag")

    def __init__(self, eng, fn, dma, tag=""):
        self.eng, self.fn, self.dma, self.tag = eng, fn, dma, tag
        self.need_inc = False
        self.incval = None
        self.waits = []


class Prog:

    def __init__(self, nc, stack):
        self.nc = nc
        self.stack = stack
        self.phase = 0
        self.streams = {e: [] for e in ENGINES}
        self.writer = {}
        self.readers = {}
        self.sem_handles = {}
        self.n_ops = 0
        self.dma_seq = {e: 0 for e in ENGINES}
        self.DMA_DEPTH = 8

    def op(self, eng, fn, reads=(), writes=(), dma=False, tag=""):
        o = _Op(eng, fn, dma, tag)
        if dma:
            o.need_inc = True
            seq = self.dma_seq[eng]
            self.dma_seq[eng] = seq + 1
            slot = seq % self.DMA_DEPTH
            use = seq // self.DMA_DEPTH + 1
            o.incval = (f"d{eng}{slot}", 16 * use)
            if use > 1:
                o.waits.append(("sem", f"d{eng}{slot}", 16 * (use - 1)))
        self.streams[eng].append(o)
        self.n_ops += 1
        deps = []
        for b in reads:
            deps.extend(self.writer.get(b, ()))
            rkey = id(o) if dma else (eng, False)
            self.readers.setdefault(b, {})[rkey] = o
        for b in writes:
            ws = self.writer.get(b, [])
            deps.extend(ws)
            rs = self.readers.get(b, {})
            deps.extend(rs.values())
            if rs:
                self.writer[b] = [o]
            else:
                ws = list(ws)
                ws.append(o)
                self.writer[b] = ws
            self.readers[b] = {}
        seen = set()
        for dep in deps:
            if dep is o or id(dep) in seen:
                continue
            seen.add(id(dep))
            if dep.eng == eng == "tensor" and not dep.dma and not dma:
                continue
            if dep.dma:
                o.waits.append(("sem", dep.incval[0], dep.incval[1]))
            else:
                dep.need_inc = True
                o.waits.append(("op", dep))
        return o

    def _sem(self, name):
        if name not in self.sem_handles:
            self.sem_handles[name] = self.stack.enter_context(self.nc.semaphore(name))
        return self.sem_handles[name]

    def flush(self):
        ph = self.phase
        self.phase += 1
        streams, self.streams = self.streams, {e: [] for e in ENGINES}
        self.writer = {}
        self.readers = {}

        counters = {}
        for eng in ENGINES:
            for o in streams[eng]:
                if o.need_inc and not o.dma:
                    name = f"s{ph}c{eng}"
                    counters[name] = counters.get(name, 0) + 1
                    o.incval = (name, counters[name])
        dma_final = {}
        for eng in ENGINES:
            specs = []
            n = self.dma_seq[eng]
            for slot in range(self.DMA_DEPTH):
                uses = (n - slot + self.DMA_DEPTH - 1) // self.DMA_DEPTH
                if uses > 0:
                    specs.append((f"d{eng}{slot}", 16 * uses))
                    self._sem(f"d{eng}{slot}")
            if specs:
                dma_final[eng] = specs
        for name in counters:
            self._sem(name)
        bsem = self._sem(f"bar{ph}")

        def wait_specs(o):
            for w in o.waits:
                if w[0] == "sem":
                    yield (w[1], w[2])
                else:
                    dep = w[1]
                    if dep.incval is not None:
                        yield dep.incval

        with self.nc.Block() as blk:
            for eng in ENGINES:
                def run(e, _eng=eng, _ops=streams[eng]):
                    waited = {}
                    for o in _ops:
                        for name, v in wait_specs(o):
                            if v > waited.get(name, 0):
                                e.wait_ge(self._sem(name), v)
                                waited[name] = v
                        r = o.fn(e)
                        if o.need_inc:
                            r.then_inc(self._sem(o.incval[0]), 16 if o.dma else 1)
                    for name, v in dma_final.get(_eng, ()):
                        if v > waited.get(name, 0):
                            e.wait_ge(self._sem(name), v)
                            waited[name] = v
                    e.nop().then_inc(bsem, 1)
                    e.wait_ge(bsem, 5)
                getattr(blk, eng)(run)


def ts(i, n=128):
    return slice(i * n, (i + 1) * n)


def build_nc():
    nc = bass.Bass("TRN2", target_bir_lowering=False)
    stack = contextlib.ExitStack()
    sb = stack.enter_context

    x = nc.dram_tensor("x", [T, D], F32, kind="ExternalInput").ap()
    wqkv = nc.dram_tensor("wqkv", [D, 3 * 1024], F32, kind="ExternalInput").ap()
    wout = nc.dram_tensor("wout", [1024, D], F32, kind="ExternalInput").ap()
    xres = nc.dram_tensor("xres", [MYT, D], F32, kind="ExternalInput").ap()
    bout = nc.dram_tensor("bout", [D], F32, kind="ExternalInput").ap()
    wg = nc.dram_tensor("wg", [D, H], F32, kind="ExternalInput").ap()
    wv_ = nc.dram_tensor("wv", [D, H], F32, kind="ExternalInput").ap()
    wo2 = nc.dram_tensor("wo2", [H, D], F32, kind="ExternalInput").ap()
    bg = nc.dram_tensor("bg", [H], F32, kind="ExternalInput").ap()
    bv = nc.dram_tensor("bv", [H], F32, kind="ExternalInput").ap()
    bo2 = nc.dram_tensor("bo2", [D], F32, kind="ExternalInput").ap()
    out = nc.dram_tensor("out", [MYT, D], F32, kind="ExternalOutput").ap()

    xb_s = nc.dram_tensor("xb_s", [T, D], BF16, kind="Internal").ap()
    qk_s = nc.dram_tensor("qk_s", [2048, T], BF16, kind="Internal").ap()
    v_s = nc.dram_tensor("v_s", [NH, 128, T], BF16, kind="Internal").ap()
    partial_s = nc.dram_tensor("partial_s", [2, 2, MYT // 2, D], F32,
                               kind="Internal").ap()
    rs_out = nc.dram_tensor("rs_out", [MYT, D], F32, kind="Internal").ap()
    y_s = nc.dram_tensor("y_s", [MYT, D], F32, kind="Internal").ap()
    nx_s = nc.dram_tensor("nx_s", [MYT, D], BF16, kind="Internal").ap()
    h_s = nc.dram_tensor("h_s", [64, 128, MYT], F32, kind="Internal").ap()
    wgt_s = nc.dram_tensor("wgt_s", [D, H], BF16, kind="Internal").ap()
    wvt_s = nc.dram_tensor("wvt_s", [D, H], BF16, kind="Internal").ap()
    wo2t_s = nc.dram_tensor("wo2t_s", [H, D], BF16, kind="Internal").ap()
    rb = nc.dram_tensor("row_bounce", [4096], F32, kind="Internal").ap()
    wqkvb_s = nc.dram_tensor("wqkvb_s", [2048, 2048], BF16, kind="Internal").ap()

    taps = dict(xb_s=xb_s, qk_s=qk_s, v_s=v_s, partial_s=partial_s,
                rs_out=rs_out, y_s=y_s, nx_s=nx_s, h_s=h_s)

    p = Prog(nc, stack)
    O = p.op

    ident = sb(nc.sbuf_tensor("ident", [128, 128], F32)).ap()
    ones_col_b = sb(nc.sbuf_tensor("ones_col_b", [128, 1], BF16)).ap()
    ones_col_f = sb(nc.sbuf_tensor("ones_col_f", [128, 1], F32)).ap()
    ones_row_b = sb(nc.sbuf_tensor("ones_row_b", [1, 128], BF16)).ap()
    ones_row_f = sb(nc.sbuf_tensor("ones_row_f", [1, 128], F32)).ap()
    bg_cols = sb(nc.sbuf_tensor("bg_cols", [128, 64], F32)).ap()
    bv_cols = sb(nc.sbuf_tensor("bv_cols", [128, 64], F32)).ap()
    gam = sb(nc.sbuf_tensor("gam", [128, 12], F32)).ap()
    gx_cols = sb(nc.sbuf_tensor("gx_cols", [128, 16], F32)).ap()
    sh_cols = sb(nc.sbuf_tensor("sh_cols", [128, 16], F32)).ap()
    ghn_cols = sb(nc.sbuf_tensor("ghn_cols", [128, 16], F32)).ap()
    p01 = contextlib.ExitStack()
    stack.callback(p01.close)
    tps = p01.enter_context(nc.psum_tensor("tps", [128, 512], F32)).ap()
    taps["gam"] = gam

    O("gpsimd", lambda g: g.memset(ident[:], 1.0), writes=["ident"])
    O("gpsimd", lambda g: g.affine_select(
        ident[:], ident[:], [[1, 128]], ALU.is_equal, 0.0, channel_multiplier=-1),
        reads=["ident"], writes=["ident"])
    O("gpsimd", lambda g: g.memset(ones_col_b[:], 1.0), writes=["ocb"])
    O("gpsimd", lambda g: g.memset(ones_col_f[:], 1.0), writes=["ocf"])
    O("gpsimd", lambda g: g.memset(ones_row_b[:], 1.0), writes=["orb"])
    O("gpsimd", lambda g: g.memset(ones_row_f[:], 1.0), writes=["orf"])
    O("gpsimd", lambda g: g.memset(gam[:], 0.0), writes=["gam"])
    for i in range(4):
        O("gpsimd", lambda g, _i=i: g.dma_start(
            xb_s[ts(_i, 1024), :], x[ts(_i, 1024), :]),
            writes=[f"xb_s{i}"], dma=True)
        O("gpsimd", lambda g, _i=i: g.dma_start(
            wqkvb_s[:, ts(_i, 512)], wqkv[:, ts(_i, 512)]),
            writes=[f"wqkvb{i}"], dma=True, tag="wqk_cast")
    if STOP_AFTER == "P0":
        _maybe_taps(nc, p, taps, "P0")
        p.flush()
        return nc, stack

    with contextlib.ExitStack() as ph:
        pb = ph.enter_context
        xfm = [pb(nc.sbuf_tensor(f"xfm{i}", [128, 16 * 1024], BF16)).ap() for i in range(2)]
        wcol4 = [pb(nc.sbuf_tensor(f"wcol4_{i}", [128, 16 * 512], BF16)).ap()
                 for i in range(3)]
        wv_sb = pb(nc.sbuf_tensor("wv_sb", [128, 16 * 1024], BF16)).ap()
        qkstg = [pb(nc.sbuf_tensor(f"qkstg{i}", [128, 1024], BF16)).ap() for i in range(2)]
        vstg = [pb(nc.sbuf_tensor(f"vstg{i}", [128, 1024], BF16)).ap() for i in range(4)]
        pp_st = [pb(nc.sbuf_tensor(f"pp_st{i}", [128, 2048], F32)).ap() for i in range(2)]
        pp_acc = pb(nc.sbuf_tensor("pp_acc", [128, 2], F32)).ap()
        gsum = pb(nc.sbuf_tensor("gsum", [128, 3], F32)).ap()
        ps = [pb(nc.psum_tensor(f"p1_ps{i}", [128, 512], F32)).ap() for i in range(4)]
        qk_ps = ps
        v_ps = ps

        O("gpsimd", lambda g: g.memset(gsum[:], 0.0), writes=["gsum"])
        for e in range(16):
            O("gpsimd", lambda g, _e=e: g.dma_start(
                wv_sb[:, ts(_e, 1024)], wqkv[ts(_e), 2048:3072]),
                writes=["wv_sb"], dma=True)

        pp_jobs = []
        for mi, w in enumerate((wg, wv_)):
            for r_ in range(16):
                for cc in range(4):
                    pp_jobs.append((mi, w[ts(r_), ts(cc, 2048)]))
        for r_ in range(64):
            pp_jobs.append((2, wo2[ts(r_), :]))
        pp_n = len(pp_jobs)
        pp_i = [0]

        def pp_step():
            if pp_i[0] >= pp_n:
                return
            i = pp_i[0]
            pp_i[0] += 1
            mi, sl = pp_jobs[i]
            b = i % 2
            O(("sync", "gpsimd")[i % 2],
              lambda s, _sl=sl, _b=b: s.dma_start(pp_st[_b][:], _sl),
              reads=[f"ppr{b}"], writes=[f"pps{b}"], dma=True, tag="pp_load")
            O("vector", lambda v, _b=b: v.reduce_sum(
                pp_acc[:, _b:_b + 1], pp_st[_b][:], AX, apply_absolute_value=True),
                reads=[f"pps{b}"], writes=[f"ppa{b}", f"ppr{b}"], tag="pp_abs")
            O("vector", lambda v, _b=b, _m=mi: v.tensor_tensor(
                gsum[:, _m:_m + 1], gsum[:, _m:_m + 1], pp_acc[:, _b:_b + 1], ALU.add),
                reads=[f"ppa{b}", "gsum"], writes=["gsum"], tag="pp_add")

        def load_xfm(g):
            for e in range(16):
                O("scalar", lambda s, _g=g, _e=e: s.dma_start(
                    xfm[_g % 2][:, ts(_e, 1024)], xb_s[ts(_g, 1024), ts(_e)],
                    transpose=True),
                    reads=[f"xb_s{g}"], writes=[f"xfm{g % 2}"], dma=True, tag="xfm")

        def load_wc4(j):
            f4 = j % 4
            O("gpsimd", lambda gg, _f4=f4, _wb=wcol4[j % 3]: gg.dma_start(
                _wb[:].rearrange("p (et f) -> p et f", et=16),
                wqkvb_s[:, ts(_f4, 512)].rearrange("(et p) f -> p et f", p=128)),
                reads=[f"wqkvb{f4}"], writes=[f"wc4_{j % 3}"], dma=True, tag="wcol")

        slot = [0]

        def pp_slot():
            n = 3 if slot[0] < 48 else 2
            slot[0] += 1
            for _ in range(n):
                pp_step()

        load_xfm(0)
        load_wc4(0)
        load_wc4(1)
        for g in range(4):
            if g + 1 < 4:
                load_xfm(g + 1)
            xf = xfm[g % 2]
            xk = f"xfm{g % 2}"
            for f in range(16):
                j = g * 4 + f // 4
                wb = wcol4[j % 3]
                wk = f"wc4_{j % 3}"
                if f % 4 == 0 and j + 2 < 16:
                    load_wc4(j + 2)
                for half in range(2):
                    bank = qk_ps[2 * (f % 2) + half]
                    bk = f"p1ps{2 * (f % 2) + half}"
                    for e in range(16):
                        O("tensor", lambda t, _e=e, _f=f, _h=half, _b=bank, _xf=xf, _wb=wb: t.matmul(
                            _b[:], _wb[:, _e * 512 + (_f % 4) * 128: _e * 512 + (_f % 4 + 1) * 128],
                            _xf[:, _e * 1024 + _h * 512: _e * 1024 + (_h + 1) * 512],
                            start=(_e == 0), stop=(_e == 15)),
                            reads=[wk, xk], writes=[bk], tag="qk_mm")
                    st = qkstg[f % 2]
                    sk = f"qkstg{f % 2}"
                    scale = QSCALE if f < 8 else 1.0
                    O("scalar", lambda s, _b=bank, _st=st, _h=half, _sc=scale: s.activation(
                        _st[:, _h * 512:(_h + 1) * 512], _b[:], AF.Copy,
                        bias=0.0, scale=_sc),
                        reads=[bk], writes=[sk], tag="qk_ev")
                O("sync", lambda s, _f=f, _g=g, _st=qkstg[f % 2]: s.dma_start(
                    qk_s[ts(_f), ts(_g, 1024)], _st[:]),
                    reads=[sk], writes=["qk_s"], dma=True, tag="qk_st")
                pp_slot()
            for half in range(2):
                for sub in range(8):
                    bank = v_ps[2 * (sub % 2) + half]
                    bk = f"p1ps{2 * (sub % 2) + half}"
                    for e in range(16):
                        O("tensor", lambda t, _s=sub, _e=e, _h=half, _b=bank, _xf=xf: t.matmul(
                            _b[:], _xf[:, _e * 1024 + _s * 128: _e * 1024 + (_s + 1) * 128],
                            wv_sb[:, _e * 1024 + _h * 512: _e * 1024 + (_h + 1) * 512],
                            start=(_e == 0), stop=(_e == 15)),
                            reads=["wv_sb", xk], writes=[bk], tag="v_mm")
                    for j in range(4):
                        O("scalar", lambda s, _b=bank, _s2=sub, _j=j: s.activation(
                            vstg[_j][:, _s2 * 128:(_s2 + 1) * 128],
                            _b[:, _j * 128:(_j + 1) * 128],
                            AF.Copy, bias=0.0, scale=1.0),
                            reads=[bk], writes=[f"vstg{j}"], tag="v_ev")
                    if sub % 2 == 1:
                        pp_slot()
                for j in range(4):
                    hd = half * 4 + j
                    O("scalar", lambda s, _g=g, _hd=hd, _j=j: s.dma_start(
                        v_s[_hd, :, ts(_g, 1024)], vstg[_j][:]),
                        reads=[f"vstg{j}"], writes=["v_s"], dma=True, tag="v_st")

        while pp_i[0] < pp_n:
            pp_step()

        for mi in range(3):
            O("tensor", lambda t, _m=mi: t.matmul(
                tps[0:1, _m:_m + 1], ones_col_f[:], gsum[:, _m:_m + 1],
                start=True, stop=True),
                reads=["gsum", "ocf"], writes=["tps"], tag="g_mm")
        O("vector", lambda v: v.tensor_scalar(
            gam[0:1, 9:12], tps[0:1, 0:3], 1.0 / float(D * H), 1e-5, ALU.mult, ALU.max),
            reads=["tps"], writes=["gamrow"], tag="g_sc")
        for mi in range(3):
            O("tensor", lambda t, _m=mi: t.matmul(
                tps[:, 384 + _m:385 + _m], ones_row_f[:], gam[0:1, 9 + _m:10 + _m],
                start=True, stop=True),
                reads=["gamrow", "orf"], writes=["tps"], tag="g_bc")
            O("vector", lambda v, _m=mi: v.tensor_scalar(
                gam[:, 3 * _m:3 * _m + 1], tps[:, 384 + _m:385 + _m], 0.5, None, ALU.mult),
                reads=["tps"], writes=["gam"], tag="g_thr")
            O("vector", lambda v, _m=mi: v.tensor_scalar(
                gam[:, 3 * _m + 1:3 * _m + 2], tps[:, 384 + _m:385 + _m], -0.5, None,
                ALU.mult), reads=["tps"], writes=["gam"], tag="g_nthr")
            O("vector", lambda v, _m=mi: v.tensor_copy(
                gam[:, 3 * _m + 2:3 * _m + 3], tps[:, 384 + _m:385 + _m]),
                reads=["tps"], writes=["gam"], tag="g_gw")

        _maybe_taps(nc, p, taps, "P1")
        p.flush()
        if STOP_AFTER == "P1":
            return nc, stack

    p01.close()
    with contextlib.ExitStack() as p23:
        ofm = p23.enter_context(
            nc.sbuf_tensor("ofm", [128, NH * T], BF16)).ap()
        wout_sb = p23.enter_context(
            nc.sbuf_tensor("wout_sb", [128, 8 * 2048], BF16)).ap()
        tw_st = [p23.enter_context(nc.sbuf_tensor(f"tw_st{i}", [128, 2048], F32)).ap()
                 for i in range(2)]
        tw_t1 = p23.enter_context(nc.sbuf_tensor("tw_t1", [128, 2048], F32)).ap()
        tw_o = [p23.enter_context(nc.sbuf_tensor(f"tw_o{i}", [128, 2048], BF16)).ap()
                for i in range(2)]
        with contextlib.ExitStack() as ph:
            pb = ph.enter_context
            qh = [pb(nc.sbuf_tensor(f"qh{i}", [128, T], BF16)).ap() for i in range(2)]
            kh = [pb(nc.sbuf_tensor(f"kh{i}", [128, T], BF16)).ap() for i in range(2)]
            vh = [pb(nc.sbuf_tensor(f"vh{i}", [128, T], BF16)).ap() for i in range(2)]
            pt = [pb(nc.sbuf_tensor(f"pt{i}", [128, 512], BF16)).ap() for i in range(3)]
            masks = pb(nc.sbuf_tensor("masks", [128, 4 * 512], F32)).ap()
            q2big = pb(nc.sbuf_tensor("q2big", [128, T], BF16)).ap()
            nsq = pb(nc.sbuf_tensor("nsq", [1, 24], F32)).ap()
            negmcol = pb(nc.sbuf_tensor("negmcol", [128, 1], F32)).ap()
            linv = pb(nc.sbuf_tensor("linv", [1, 512], F32)).ap()
            bcb = pb(nc.sbuf_tensor("bcb", [128, 512], F32)).ap()
            s_ps = [pb(nc.psum_tensor(f"s_ps{i}", [128, 512], F32)).ap() for i in range(3)]
            o_ps = [pb(nc.psum_tensor(f"o_ps{i}", [128, 512], F32)).ap() for i in range(2)]
            l_ps = [pb(nc.psum_tensor(f"l_ps{i}", [128, 512], F32)).ap() for i in range(2)]
            bc_ps = pb(nc.psum_tensor("bc_ps", [128, 512], F32)).ap()

            tern_jobs = []
            for mi_, (w_, c_) in enumerate(((wg, wgt_s), (wv_, wvt_s), (wo2, wo2t_s))):
                rows = D if mi_ < 2 else H
                for r_ in range(rows // 128):
                    for cc_ in range(w_.shape[1] // 2048):
                        tern_jobs.append((mi_, w_[ts(r_), ts(cc_, 2048)],
                                          c_[ts(r_), ts(cc_, 2048)]))
            tern_i = [0]

            def tern_step(st_eng="gpsimd"):
                if tern_i[0] >= len(tern_jobs):
                    return
                i = tern_i[0]
                tern_i[0] += 1
                mi_, wsl, csl = tern_jobs[i]
                bb_ = i % 2
                O("sync", lambda s, _w=wsl, _b=bb_: s.dma_start(tw_st[_b][:], _w),
                    reads=[f"twr{bb_}"], writes=[f"tws{bb_}"], dma=True, tag="tw_ld")
                O("vector", lambda v, _b=bb_, _m=mi_: v.tensor_scalar(
                    tw_t1[:], tw_st[_b][:], gam[:, 3 * _m:3 * _m + 1], None,
                    ALU.is_gt),
                    reads=[f"tws{bb_}", "gam"], writes=["twt1"], tag="tw_gt")
                O("vector", lambda v, _b=bb_, _m=mi_: v.scalar_tensor_tensor(
                    tw_o[_b][:], tw_st[_b][:], gam[:, 3 * _m + 1:3 * _m + 2],
                    tw_t1[:], ALU.is_lt, ALU.subtract),
                    reads=[f"tws{bb_}", "twt1", "gam"],
                    writes=[f"two{bb_}", f"twr{bb_}"], tag="tw_tern")
                O(st_eng, lambda g, _csl=csl, _b=bb_: g.dma_start(_csl, tw_o[_b][:]),
                    reads=[f"two{bb_}"], writes=["terncache"], dma=True, tag="tw_st")

            O("gpsimd", lambda g: g.memset(masks[:], 0.0), writes=["masks"])
            for r_ in range(4):
                O("gpsimd", lambda g, _r=r_: g.affine_select(
                    masks[:, ts(_r, 512)], masks[:, ts(_r, 512)], [[1, 512]],
                    ALU.is_ge, -1e30, base=-128 * _r, channel_multiplier=-1),
                    reads=["masks"], writes=["masks"])

            def load_head(h):
                b = h % 2
                O("sync", lambda s, _h=h, _b=b: s.dma_start(qh[_b][:], qk_s[ts(_h), :]),
                  reads=["qk_s"], writes=[f"qh{b}"], dma=True, tag="ld_q")
                O("sync", lambda s, _h=h, _b=b: s.dma_start(
                    kh[_b][:], qk_s[1024 + _h * 128:1024 + (_h + 1) * 128, :]),
                    reads=["qk_s"], writes=[f"kh{b}"], dma=True, tag="ld_k")
                O("sync", lambda s, _h=h, _b=b: s.dma_start(vh[_b][:], v_s[_h, :, :]),
                  reads=["v_s"], writes=[f"vh{b}"], dma=True, tag="ld_v")

            load_head(0)
            for h in range(NH):
                if h + 1 < NH:
                    load_head(h + 1)
                b = h % 2
                qk_, kk_, vk_ = f"qh{b}", f"kh{b}", f"vh{b}"
                for side, buf, bk_ in ((0, qh[b], qk_), (1, kh[b], kk_)):
                    O("scalar", lambda sc, _buf=buf: sc.activation(
                        q2big[:], _buf[:], AF.Square, bias=0.0, scale=1.0),
                        reads=[bk_], writes=["q2big"], tag="cs_sq")
                    for c in range(8):
                        O("tensor", lambda t, _c=c: t.matmul(
                            s_ps[_c % 3][0:1, :], ones_col_b[:], q2big[:, ts(_c, 512)],
                            start=True, stop=True),
                            reads=["q2big", "ocb"], writes=[f"s_ps{c % 3}"], tag="cs_mm")
                        O("vector", lambda v, _side=side, _c=c: v.reduce_max(
                            nsq[0:1, 8 * _side + _c:8 * _side + _c + 1],
                            s_ps[_c % 3][0:1, :], AX),
                            reads=[f"s_ps{c % 3}"], writes=["nsq"], tag="cs_red")
                O("vector", lambda v: v.reduce_max(nsq[0:1, 16:17], nsq[0:1, 0:8], AX),
                  reads=["nsq"], writes=["nsq"], tag="cs_mq")
                O("vector", lambda v: v.reduce_max(nsq[0:1, 17:18], nsq[0:1, 8:16], AX),
                  reads=["nsq"], writes=["nsq"], tag="cs_mk")
                O("vector", lambda v: v.tensor_tensor(
                    nsq[0:1, 18:19], nsq[0:1, 16:17], nsq[0:1, 17:18], ALU.mult),
                    reads=["nsq"], writes=["nsq"], tag="cs_prod")
                O("scalar", lambda sc: sc.activation(
                    nsq[0:1, 19:20], nsq[0:1, 18:19], AF.Ln, bias=0.0, scale=1.0),
                    reads=["nsq"], writes=["nsq2"], tag="cs_ln")
                O("scalar", lambda sc: sc.activation(
                    nsq[0:1, 20:21], nsq[0:1, 19:20], AF.Exp, bias=0.0, scale=0.5),
                    reads=["nsq2"], writes=["nsq3"], tag="cs_exp")
                O("vector", lambda v: v.tensor_scalar(
                    nsq[0:1, 21:22], nsq[0:1, 20:21], -1.0, None, ALU.mult),
                    reads=["nsq3"], writes=["nsq4"], tag="cs_neg")
                O("tensor", lambda t: t.matmul(
                    bc_ps[:, 0:1], ones_row_f[:], nsq[0:1, 21:22], start=True, stop=True),
                    reads=["nsq4", "orf"], writes=["bc_ps"], tag="cs_bc")
                O("vector", lambda v: v.tensor_copy(negmcol[:], bc_ps[:, 0:1]),
                  reads=["bc_ps"], writes=["negmcol"], tag="cs_col")

                steps = [(g, kt) for g in range(8) for kt in range(4 * g + 4)]
                ns = len(steps)

                def emit_S(i):
                    g, kt = steps[i]
                    sb_ = i % 3
                    O("tensor", lambda t, _kt=kt, _g=g, _s=sb_, _kh=kh[b], _qh=qh[b]: t.matmul(
                        s_ps[_s][:], _kh[:, ts(_kt)], _qh[:, ts(_g, 512)],
                        start=True, stop=True),
                        reads=[kk_, qk_], writes=[f"s_ps{sb_}"], tag="st_mm")
                    if kt >= 4 * g:
                        r = kt - 4 * g
                        O("vector", lambda v, _s=sb_, _r=r: v.tensor_tensor(
                            s_ps[_s][:], s_ps[_s][:], masks[:, ts(_r, 512)], ALU.add),
                            reads=[f"s_ps{sb_}", "masks"], writes=[f"s_ps{sb_}"],
                            tag="st_mask")

                def emit_exp(i):
                    sb_ = i % 3
                    O("scalar", lambda sc, _s=sb_: sc.activation(
                        pt[_s][:], s_ps[_s][:], AF.Exp, bias=negmcol[:], scale=1.0),
                        reads=[f"s_ps{sb_}", "negmcol"], writes=[f"pt{sb_}"],
                        tag="st_exp")

                def emit_AVl(i):
                    g, kt = steps[i]
                    sb_ = i % 3
                    ob = g % 2
                    nk = 4 * g + 4
                    O("tensor", lambda t, _kt=kt, _ob=ob, _s=sb_, _vh=vh[b], _nk=nk: t.matmul(
                        o_ps[_ob][:], _vh[:, ts(_kt)], pt[_s][:],
                        start=(_kt == 0), stop=(_kt == _nk - 1)),
                        reads=[vk_, f"pt{sb_}"], writes=[f"o_ps{ob}"], tag="av_mm")
                    O("tensor", lambda t, _kt=kt, _ob=ob, _s=sb_, _nk=nk: t.matmul(
                        l_ps[_ob][0:1, :], ones_col_b[:], pt[_s][:],
                        start=(_kt == 0), stop=(_kt == _nk - 1)),
                        reads=["ocb", f"pt{sb_}"], writes=[f"l_ps{ob}"], tag="l_mm")

                def emit_fin(g):
                    ob = g % 2
                    O("vector", lambda v, _ob=ob: v.reciprocal(linv[:], l_ps[_ob][0:1, :]),
                      reads=[f"l_ps{ob}"], writes=["linv"], tag="o_linv")
                    O("tensor", lambda t: t.matmul(
                        bc_ps[:], ones_row_f[:], linv[:], start=True, stop=True),
                        reads=["linv", "orf"], writes=["bc_ps"], tag="o_bc")
                    O("vector", lambda v: v.tensor_copy(bcb[:], bc_ps[:]),
                      reads=["bc_ps"], writes=["bcb"], tag="o_bcc")
                    O("vector", lambda v, _ob=ob, _h=h, _g=g: v.tensor_tensor(
                        ofm[:, _h * T + _g * 512:_h * T + (_g + 1) * 512],
                        o_ps[_ob][:], bcb[:], ALU.mult),
                        reads=[f"o_ps{ob}", "bcb"], writes=["ofm"], tag="o_mul")

                emit_S(0)
                if ns > 1:
                    emit_S(1)
                pending = []
                for i in range(ns):
                    emit_exp(i)
                    if i + 2 < ns:
                        emit_S(i + 2)
                    emit_AVl(i)
                    while pending and pending[0][0] <= i - 2:
                        emit_fin(pending.pop(0)[1])
                        if h >= 1:
                            tern_step()
                            tern_step()
                            tern_step()
                    g, kt = steps[i]
                    if kt == 4 * g + 3:
                        pending.append((i, g))
                while pending:
                    emit_fin(pending.pop(0)[1])
                if 2 <= h <= 5:
                    for i_ in range(2):
                        et = (h - 2) * 2 + i_
                        O("gpsimd", lambda g_, _et=et: g_.dma_start(
                            wout_sb[:, ts(_et, 2048)], wout[ts(_et), :]),
                            writes=["wout_sb"], dma=True, tag="ld_wout")

            _maybe_taps(nc, p, taps, "P2")
            p.flush()
            if STOP_AFTER == "P2":
                return nc, stack

        with contextlib.ExitStack() as ph:
            pb = ph.enter_context
            pst = [pb(nc.sbuf_tensor(f"pst{i}", [128, 512], F32)).ap() for i in range(3)]
            pp_ps = [pb(nc.psum_tensor(f"pp_ps{i}", [128, 512], F32)).ap() for i in range(6)]
            brow2 = pb(nc.sbuf_tensor("brow2", [1, H], F32)).ap()
            brov2 = pb(nc.sbuf_tensor("brov2", [1, H], F32)).ap()
            bps = pb(nc.psum_tensor("bps", [128, 16], F32)).ap()
            O("scalar", lambda s: s.dma_start(
                brow2[:], bg.rearrange("(a b) -> a b", a=1)),
                writes=["brow2"], dma=True)
            O("scalar", lambda s: s.dma_start(
                brov2[:], bv.rearrange("(a b) -> a b", a=1)),
                writes=["brov2"], dma=True)

            def bias_batch(bt):
                src, sk_, dstc, dk_ = ((brow2, "brow2", bg_cols, "bgc") if bt < 8
                                       else (brov2, "brov2", bv_cols, "bvc"))
                hb_ = bt % 2
                for q in range(8):
                    j = (bt % 8) * 8 + q
                    O("tensor", lambda t, _j=j, _h=hb_, _q=q, _s=src: t.matmul(
                        bps[:, _h * 8 + _q:_h * 8 + _q + 1], _s[:, ts(_j)],
                        ones_row_f[:, 0:1], start=True, stop=True),
                        reads=[sk_, "orf"], writes=[f"bps{hb_}"], tag="b_mm")
                O("vector", lambda v, _bt=bt, _h=hb_, _d=dstc: v.tensor_copy(
                    _d[:, (_bt % 8) * 8:(_bt % 8) * 8 + 8], bps[:, _h * 8:_h * 8 + 8]),
                    reads=[f"bps{hb_}"], writes=[dk_], tag="b_cp")

            ccs1 = p.stack.enter_context(nc.semaphore("ccsem1"))
            ccs2 = p.stack.enter_context(nc.semaphore("ccsem2"))

            def _cc_issue(g, j, sem):
                g.collective_compute(
                    "ReduceScatter", ALU.add,
                    replica_groups=[[0, 1], [2, 3], [4, 5], [6, 7]],
                    ins=[partial_s[j].rearrange("a b c -> (a b) c")],
                    outs=[rs_out[j * 1024:(j + 1) * 1024, :]],
                ).then_inc(sem)

            tt_order = list(range(0, 8)) + list(range(16, 24))                 + list(range(8, 16)) + list(range(24, 32))
            for ti, tt in enumerate(tt_order):
                half = 0 if (tt % 16) < 8 else 1
                tern_step()
                tern_step()
                if ti % 2 == 0:
                    bias_batch(ti // 2)
                if ti == 16:
                    O("gpsimd", lambda g: _cc_issue(g, 0, ccs1),
                      reads=["part0"], tag="rs1_issue")
                for ec in range(4):
                    bi = (tt * 4 + ec) % 6
                    bank = pp_ps[bi]
                    bk = f"pp_ps{bi}"
                    for et in range(8):
                        O("tensor", lambda t, _tt=tt, _ec=ec, _et=et, _b=bank: t.matmul(
                            _b[:], ofm[:, _et * T + _tt * 128:_et * T + (_tt + 1) * 128],
                            wout_sb[:, _et * 2048 + _ec * 512:_et * 2048 + (_ec + 1) * 512],
                            start=(_et == 0), stop=(_et == 7)),
                            reads=["ofm", "wout_sb"], writes=[bk], tag="op_mm")
                    sti = (tt * 4 + ec) % 3
                    O("scalar", lambda s, _b=bank, _st=pst[sti]: s.activation(
                        _st[:], _b[:], AF.Copy, bias=0.0, scale=1.0),
                        reads=[bk], writes=[f"pst{sti}"], tag="op_ev")
                    O("sync", lambda s, _tt=tt, _ec=ec, _h=half, _st=pst[sti]: s.dma_start(
                        partial_s[_h, _tt // 16, ts(_tt % 8), ts(_ec, 512)], _st[:]),
                        reads=[f"pst{sti}"], writes=[f"part{half}"], dma=True,
                        tag="op_st")
            O("gpsimd", lambda g: _cc_issue(g, 1, ccs2),
              reads=["part1"], tag="rs2_issue")
            while tern_i[0] < len(tern_jobs):
                tern_step(st_eng="scalar")
            O("gpsimd", lambda g: g.wait_ge(ccs1, 1), writes=["rs0"],
              tag="rs1_wait")
            _maybe_taps(nc, p, taps, "P3a")
            p.flush()
            if STOP_AFTER == "P3a":
                return nc, stack


    with contextlib.ExitStack() as p45:
        runmax = p45.enter_context(nc.sbuf_tensor("runmax", [128, MYT], F32)).ap()
        runmin = p45.enter_context(nc.sbuf_tensor("runmin", [128, MYT], F32)).ap()

        p34 = contextlib.ExitStack()
        p45.callback(p34.close)
        nx_fm = p34.enter_context(nc.sbuf_tensor("nx_fm", [128, 16 * 2048], BF16)).ap()
        sgbc = p34.enter_context(nc.sbuf_tensor("sgbc", [128, MYT], F32)).ap()
        svbc = sgbc

        with contextlib.ExitStack() as ph:
            pb = ph.enter_context
            rsld = [pb(nc.sbuf_tensor(f"rsld{i}", [128, D], F32)).ap() for i in range(2)]
            xr = [pb(nc.sbuf_tensor(f"xr{i}", [128, D], F32)).ap() for i in range(2)]
            bbrow = pb(nc.sbuf_tensor("bbrow", [1, D], F32)).ap()
            bbbc = pb(nc.sbuf_tensor("bbbc", [128, D], F32)).ap()
            yt = pb(nc.sbuf_tensor("yt", [128, D], F32)).ap()
            sq = pb(nc.sbuf_tensor("sq", [128, D], F32)).ap()
            nxt = pb(nc.sbuf_tensor("nxt", [128, D], BF16)).ap()
            stat = pb(nc.sbuf_tensor("stat", [128, 16], F32)).ap()
            srow = pb(nc.sbuf_tensor("srow", [1, 512], F32)).ap()
            tps = pb(nc.psum_tensor("tps3", [128, 512], F32)).ap()

            O("gpsimd", lambda g: g.wait_ge(ccs2, 1), writes=["rs1"],
              tag="rs2_wait")
            O("sync", lambda s: s.dma_start(bbrow[:], bout.rearrange("(a b) -> a b", a=1)),
              writes=["bbrow"], dma=True, tag="bb_ld")
            for c in range(4):
                O("tensor", lambda t, _c=c: t.matmul(
                    tps[:, 0:512], ones_row_f[:], bbrow[:, ts(_c, 512)],
                    start=True, stop=True),
                    reads=["bbrow", "orf"], writes=["tps"], tag="bb_mm")
                O("vector", lambda v, _c=c: v.tensor_copy(bbbc[:, ts(_c, 512)],
                                                          tps[:, 0:512]),
                  reads=["tps"], writes=["bbbc"], tag="bb_cp")

            for tt in range(16):
                b = tt % 2
                O("sync", lambda s, _tt=tt, _b=b: s.dma_start(
                    rsld[_b][:], rs_out[ts(_tt), :]),
                    reads=[f"rs{0 if tt < 8 else 1}"],
                    writes=[f"rsld{b}"], dma=True, tag="rs_ld")
                O("sync", lambda s, _tt=tt, _b=b: s.dma_start(
                    xr[_b][:], xres[ts(_tt), :]),
                    writes=[f"xr{b}"], dma=True, tag="xr_ld")
                O("vector", lambda v, _b=b: v.tensor_tensor(
                    rsld[_b][:], rsld[_b][:], xr[_b][:], ALU.add),
                    reads=[f"rsld{b}", f"xr{b}"], writes=[f"rsld{b}"], tag="res_add")
                O("vector", lambda v, _b=b: v.tensor_tensor(
                    rsld[_b][:], rsld[_b][:], bbbc[:], ALU.add),
                    reads=[f"rsld{b}", "bbbc"], writes=[f"rsld{b}"], tag="bias_add")
                O("scalar", lambda s, _b=b: s.activation(
                    sq[:], rsld[_b][:], AF.Square, bias=0.0, scale=1.0,
                    accum_out=stat[:, 0:1]),
                    reads=[f"rsld{b}"], writes=["sq", "st0"], tag="ln_sq")
                O("vector", lambda v, _b=b: v.reduce_sum(stat[:, 1:2], rsld[_b][:], AX),
                  reads=[f"rsld{b}"], writes=["st1"], tag="ln_sum")
                O("vector", lambda v: v.tensor_scalar(
                    stat[:, 2:3], stat[:, 1:2], 1.0 / D, None, ALU.mult),
                    reads=["st1"], writes=["st2"], tag="ln_m")
                O("vector", lambda v: v.tensor_scalar(
                    stat[:, 3:4], stat[:, 0:1], 1.0 / D, None, ALU.mult),
                    reads=["st0"], writes=["st3"], tag="ln_ex2")
                O("scalar", lambda s: s.activation(
                    stat[:, 4:5], stat[:, 2:3], AF.Square, bias=0.0, scale=1.0),
                    reads=["st2"], writes=["st4"], tag="ln_m2")
                O("vector", lambda v: v.tensor_tensor(
                    stat[:, 5:6], stat[:, 3:4], stat[:, 4:5], ALU.subtract),
                    reads=["st3", "st4"], writes=["st5"], tag="ln_var")
                O("vector", lambda v: v.tensor_scalar(
                    stat[:, 14:15], stat[:, 5:6], EPS, None, ALU.add),
                    reads=["st5"], writes=["st14"], tag="ln_eps")
                O("scalar", lambda s: s.activation(
                    stat[:, 6:7], stat[:, 14:15], AF.Ln, bias=0.0, scale=1.0),
                    reads=["st14"], writes=["st6"], tag="ln_ln")
                O("scalar", lambda s: s.activation(
                    stat[:, 7:8], stat[:, 6:7], AF.Exp, bias=0.0, scale=-0.5),
                    reads=["st6"], writes=["st7"], tag="ln_rstd")
                O("vector", lambda v: v.tensor_tensor(
                    stat[:, 8:9], stat[:, 2:3], stat[:, 7:8], ALU.mult),
                    reads=["st2", "st7"], writes=["st8"], tag="ln_mr")
                O("vector", lambda v: v.tensor_scalar(
                    stat[:, 9:10], stat[:, 8:9], -1.0, None, ALU.mult),
                    reads=["st8"], writes=["st9"], tag="ln_nmr")
                O("vector", lambda v, _b=b: v.tensor_scalar(
                    yt[:], rsld[_b][:], stat[:, 7:8], stat[:, 9:10],
                    ALU.mult, ALU.add),
                    reads=[f"rsld{b}", "st9", "st7"], writes=["yt"], tag="ln_y")
                O("sync", lambda s, _tt=tt: s.dma_start(y_s[ts(_tt), :], yt[:]),
                  reads=["yt"], writes=["y_s"], dma=True, tag="y_st")
                O("vector", lambda v: v.reduce_max(
                    stat[:, 10:11], yt[:], AX, apply_absolute_value=True),
                    reads=["yt"], writes=["st10"], tag="qx_amax")
                O("vector", lambda v: v.tensor_scalar(
                    stat[:, 11:12], stat[:, 10:11], 1e-5, None, ALU.max),
                    reads=["st10"], writes=["st11"], tag="qx_clip")
                O("vector", lambda v: v.reciprocal(stat[:, 12:13], stat[:, 11:12]),
                  reads=["st11"], writes=["st12"], tag="qx_rec")
                O("vector", lambda v: v.tensor_scalar(
                    stat[:, 13:14], stat[:, 12:13], -127.0, None, ALU.mult),
                    reads=["st12"], writes=["st13"], tag="qx_nsc")
                O("vector", lambda v, _tt=tt: v.tensor_scalar(
                    gx_cols[:, _tt:_tt + 1], stat[:, 11:12], 1.0 / 127.0, None, ALU.mult),
                    reads=["st11"], writes=["gx_cols"], tag="qx_gx")
                O("scalar", lambda s, _b=b: s.activation(
                    sq[:], yt[:], AF.Copy, bias=0.0, scale=stat[:, 13:14]),
                    reads=["yt", "st13", "sq"], writes=["sq"], tag="qx_scale")
                O("vector", lambda v: v.tensor_scalar(
                    sq[:], sq[:], MAGIC, -MAGIC, ALU.add, ALU.add),
                    reads=["sq"], writes=["sq"], tag="qx_round")
                O("vector", lambda v: v.tensor_copy(nxt[:], sq[:]),
                    reads=["sq"], writes=["nxt"], tag="qx_cast")
                O("sync", lambda s, _tt=tt: s.dma_start(nx_s[ts(_tt), :], nxt[:]),
                  reads=["nxt"], writes=["nx_s"], dma=True, tag="nx_st")

            for cch in range(4):
                for j in range(4):
                    tt = cch * 4 + j
                    O("tensor", lambda t, _tt=tt, _j=j: t.matmul(
                        tps[0:1, ts(_j)], gx_cols[:, _tt:_tt + 1], ident[:],
                        start=True, stop=True),
                        reads=["gx_cols", "ident"], writes=["tps"], tag="sg_tr")
                O("vector", lambda v: v.tensor_copy(srow[:], tps[0:1, 0:512]),
                  reads=["tps"], writes=["srow"], tag="sg_row")
                O("tensor", lambda t: t.matmul(
                    tps[:, 0:512], ones_row_f[:], srow[:], start=True, stop=True),
                    reads=["srow", "orf"], writes=["tps"], tag="sg_bc")
                O("vector", lambda v, _c=cch: v.tensor_copy(
                    sgbc[:, ts(_c, 512)], tps[:, 0:512]),
                    reads=["tps"], writes=["sgbc"], tag="sg_cp")
            for et in range(16):
                O("sync", lambda s, _et=et: s.dma_start(
                    nx_fm[:, ts(_et, 2048)], nx_s[:, ts(_et)], transpose=True),
                    reads=["nx_s"], writes=["nx_fm"], dma=True, tag="nx_tr")
            _maybe_taps(nc, p, taps, "P3")
            p.flush()
            if STOP_AFTER == "P3":
                return nc, stack
        with contextlib.ExitStack() as ph:
            pb = ph.enter_context
            wgt = [pb(nc.sbuf_tensor(f"wgt{i}", [128, 2048], BF16)).ap()
                   for i in range(4)]
            g1 = pb(nc.sbuf_tensor("g1", [128, 512], F32)).ap()
            gate = pb(nc.sbuf_tensor("gate", [128, MYT], F32)).ap()
            v1 = pb(nc.sbuf_tensor("v1", [128, 512], F32)).ap()
            htile = [pb(nc.sbuf_tensor(f"htile{i}", [128, 512], F32)).ap()
                     for i in range(2)]
            f_ps = [pb(nc.psum_tensor(f"f_ps{i}", [128, 512], F32)).ap()
                    for i in range(4)]

            O("gpsimd", lambda g: g.memset(runmax[:], 0.0), writes=["runmax"])
            O("gpsimd", lambda g: g.memset(runmin[:], 0.0), writes=["runmin"])

            def tern_load(mat, ht, buf):
                src = wgt_s if mat == 0 else wvt_s
                O("sync", lambda s, _b=buf, _ht=ht, _w=src: s.dma_start(
                    wgt[_b][:].rearrange("p (et f) -> p et f", et=16),
                    _w[:, ts(_ht)].rearrange("(et p) f -> p et f", p=128)),
                    writes=[f"wgt{buf}"], dma=True, tag="w_ld")

            tern_load(0, 0, 0)
            tern_load(1, 0, 2)
            for ht in range(64):
                gb, vb = ht % 2, 2 + ht % 2
                if ht + 1 < 64:
                    tern_load(0, ht + 1, (ht + 1) % 2)
                    tern_load(1, ht + 1, 2 + (ht + 1) % 2)
                for mat, wb_, psoff in ((0, gb, 0), (1, vb, 2)):
                    for tc in range(4):
                        bank = f_ps[psoff + tc % 2]
                        bk = f"f_ps{psoff + tc % 2}"
                        for et in range(16):
                            O("tensor", lambda t, _w=wgt[wb_], _et=et, _tc=tc, _b=bank: t.matmul(
                                _b[:], _w[:, ts(_et)],
                                nx_fm[:, _et * 2048 + _tc * 512:_et * 2048 + (_tc + 1) * 512],
                                start=(_et == 0), stop=(_et == 15)),
                                reads=[f"wgt{wb_}", "nx_fm"], writes=[bk], tag="f_mm")
                        if mat == 0:
                            O("vector", lambda v, _b=bank, _tc=tc: v.scalar_tensor_tensor(
                                g1[:], _b[:], gam[:, 2:3], sgbc[:, ts(_tc, 512)],
                                ALU.mult, ALU.mult),
                                reads=[bk, "gam", "sgbc"], writes=["g1"], tag="g_sc")
                            O("scalar", lambda s, _ht=ht, _tc=tc: s.activation(
                                gate[:, ts(_tc, 512)], g1[:], AF.Silu,
                                bias=bg_cols[:, _ht:_ht + 1], scale=1.0),
                                reads=["g1", "bgc"], writes=["gate"], tag="g_silu")
                        else:
                            O("vector", lambda v, _b=bank, _tc=tc: v.scalar_tensor_tensor(
                                v1[:], _b[:], gam[:, 5:6], sgbc[:, ts(_tc, 512)],
                                ALU.mult, ALU.mult),
                                reads=[bk, "gam", "sgbc"], writes=["v1"], tag="v_sc")
                            hb = tc % 2
                            O("vector", lambda v, _ht=ht, _tc=tc, _hb=hb: v.scalar_tensor_tensor(
                                htile[_hb][:], v1[:], bv_cols[:, _ht:_ht + 1],
                                gate[:, ts(_tc, 512)], ALU.add, ALU.mult),
                                reads=["v1", "bvc", "gate"], writes=[f"htile{hb}"],
                                tag="h_mul")
                            O("vector", lambda v, _tc=tc, _hb=hb: v.tensor_tensor(
                                runmax[:, ts(_tc, 512)], runmax[:, ts(_tc, 512)],
                                htile[_hb][:], ALU.max),
                                reads=["runmax", f"htile{hb}"], writes=["runmax"],
                                tag="h_max")
                            O("vector", lambda v, _tc=tc, _hb=hb: v.tensor_tensor(
                                runmin[:, ts(_tc, 512)], runmin[:, ts(_tc, 512)],
                                htile[_hb][:], ALU.min),
                                reads=["runmin", f"htile{hb}"], writes=["runmin"],
                                tag="h_min")
                            O("scalar", lambda s, _ht=ht, _tc=tc, _hb=hb: s.dma_start(
                                h_s[_ht, :, ts(_tc, 512)], htile[_hb][:]),
                                reads=[f"htile{hb}"], writes=["h_s"], dma=True,
                                tag="h_st")
            _maybe_taps(nc, p, taps, "P4")
            p.flush()
            if STOP_AFTER == "P4":
                return nc, stack
        p34.close()

        with contextlib.ExitStack() as ph:
            pb = ph.enter_context
            ghrow = pb(nc.sbuf_tensor("ghrow", [128, 16], F32)).ap()
            nbc = pb(nc.sbuf_tensor("nbc", [128, MYT], F32)).ap()
            bo2bc = pb(nc.sbuf_tensor("bo2bc", [128, D], F32)).ap()
            srow2 = pb(nc.sbuf_tensor("srow2", [1, 512], F32)).ap()
            outacc = pb(nc.sbuf_tensor("outacc", [128, 4 * 2048], F32)).ap()
            wo2t = [pb(nc.sbuf_tensor(f"wo2t{i}", [128, 8 * 2048], BF16)).ap()
                    for i in range(2)]
            hstage = [pb(nc.sbuf_tensor(f"hstage{i}", [128, 8 * 512], F32)).ap()
                      for i in range(2)]
            nh_sb = [pb(nc.sbuf_tensor(f"nh_sb{i}", [128, 8 * 512], BF16)).ap()
                     for i in range(2)]
            ep = pb(nc.sbuf_tensor("ep", [128, D], F32)).ap()
            yb = [pb(nc.sbuf_tensor(f"yb{i}", [128, D], F32)).ap() for i in range(2)]
            stat2 = pb(nc.sbuf_tensor("stat2", [128, 16], F32)).ap()
            tps = pb(nc.psum_tensor("tps5", [128, 512], F32)).ap()
            x_ps = [pb(nc.psum_tensor(f"x_ps{i}", [128, 512], F32)).ap()
                    for i in range(5)]

            O("vector", lambda v: v.tensor_scalar(
                runmin[:], runmin[:], -1.0, None, ALU.mult),
                reads=["runmin"], writes=["runmin"], tag="gh_neg")
            O("vector", lambda v: v.tensor_tensor(
                runmax[:], runmax[:], runmin[:], ALU.max),
                reads=["runmax", "runmin"], writes=["runmax"], tag="gh_comb")
            for tt in range(16):
                O("tensor", lambda t, _tt=tt: t.transpose(
                    x_ps[_tt % 2][:, 0:128], runmax[:, ts(_tt)], ident[:]),
                    reads=["runmax", "ident"], writes=[f"x_ps{tt % 2}"], tag="gh_tr")
                O("vector", lambda v, _tt=tt: v.reduce_max(
                    ghrow[:, _tt:_tt + 1], x_ps[_tt % 2][:, 0:128], AX),
                    reads=[f"x_ps{tt % 2}"], writes=["ghrow"], tag="gh_red")
            O("vector", lambda v: v.tensor_scalar(
                ghrow[:], ghrow[:], 1e-5, None, ALU.max),
                reads=["ghrow"], writes=["ghrow"], tag="gh_clip")
            O("vector", lambda v: v.tensor_scalar(
                sh_cols[:], ghrow[:], gam[:, 8:9], 1.0 / 127.0, ALU.mult, ALU.mult),
                reads=["ghrow", "gam"], writes=["sh_cols"], tag="sh_mk")
            O("vector", lambda v: v.reciprocal(ghn_cols[:], ghrow[:]),
              reads=["ghrow"], writes=["ghn_cols"], tag="ghn_rec")
            O("vector", lambda v: v.tensor_scalar(
                ghn_cols[:], ghn_cols[:], -127.0, None, ALU.mult),
                reads=["ghn_cols"], writes=["ghn_cols"], tag="ghn_mul")
            for cch in range(4):
                for j in range(4):
                    tt = cch * 4 + j
                    O("tensor", lambda t, _tt=tt, _j=j: t.matmul(
                        tps[0:1, ts(_j)], ghn_cols[:, _tt:_tt + 1], ident[:],
                        start=True, stop=True),
                        reads=["ghn_cols", "ident"], writes=["tps"], tag="nb_tr")
                O("vector", lambda v: v.tensor_copy(srow2[:], tps[0:1, 0:512]),
                  reads=["tps"], writes=["srow2"], tag="nb_row")
                O("tensor", lambda t: t.matmul(
                    tps[:, 0:512], ones_row_f[:], srow2[:], start=True, stop=True),
                    reads=["srow2", "orf"], writes=["tps"], tag="nb_bc")
                O("vector", lambda v, _c=cch: v.tensor_copy(nbc[:, ts(_c, 512)],
                                                            tps[:, 0:512]),
                  reads=["tps"], writes=["nbc"], tag="nb_cp")
            for c in range(4):
                O("sync", lambda s, _c=c: s.dma_start(
                    srow2[:], bo2.rearrange("(a b) -> a b", a=1)[:, ts(_c, 512)]),
                    reads=["srow2r"], writes=["srow2"], dma=True, tag="bo2_ld")
                O("tensor", lambda t, _c=c: t.matmul(
                    tps[:, 0:512], ones_row_f[:], srow2[:],
                    start=True, stop=True),
                    reads=["srow2", "orf"], writes=["tps"], tag="bo2_mm")
                O("vector", lambda v, _c=c: v.tensor_copy(bo2bc[:, ts(_c, 512)],
                                                          tps[:, 0:512]),
                  reads=["tps", "srow2"], writes=["bo2bc", "srow2r"], tag="bo2_cp")

            def p5_loads_w(th, hc):
                b = hc % 2
                for hsub in range(8):
                    ht = hc * 8 + hsub
                    O("sync", lambda s, _ht=ht, _b=b, _hs=hsub: s.dma_start(
                        wo2t[_b][:, ts(_hs, 2048)], wo2t_s[ts(_ht), :]),
                        reads=[f"wo2r{b}"], writes=[f"wo2t{b}"], dma=True,
                        tag="o_wld")

            def p5_loads_h(th, hc):
                b = hc % 2
                for hsub in range(8):
                    ht = hc * 8 + hsub
                    O("scalar", lambda s, _ht=ht, _th=th, _b=b, _hs=hsub: s.dma_start(
                        hstage[_b][:, ts(_hs, 512)],
                        h_s[_ht, :, _th * 512:(_th + 1) * 512]),
                        reads=[f"hqr{b}"], writes=[f"hstage{b}"], dma=True,
                        tag="h_ld")

            def p5_loads(th, hc):
                p5_loads_w(th, hc)
                p5_loads_h(th, hc)

            def p5_quant(th, hc):
                b = hc % 2
                for hs in range(8):
                    O("vector", lambda v, _b=b, _th=th, _hs=hs: v.tensor_tensor(
                        hstage[_b][:, ts(_hs, 512)], hstage[_b][:, ts(_hs, 512)],
                        nbc[:, _th * 512:(_th + 1) * 512], ALU.mult),
                        reads=[f"hstage{b}", "nbc"], writes=[f"hstage{b}"], tag="h_sc")
                O("vector", lambda v, _b=b: v.tensor_scalar(
                    hstage[_b][:], hstage[_b][:], MAGIC, -MAGIC, ALU.add, ALU.add),
                    reads=[f"hstage{b}"], writes=[f"hstage{b}"], tag="h_round")
                O("vector", lambda v, _b=b: v.tensor_copy(nh_sb[_b][:], hstage[_b][:]),
                    reads=[f"hstage{b}"], writes=[f"nh_sb{b}", f"hqr{b}"],
                    tag="h_cast")

            def ep_tt(th_, tt):
                gt = th_ * 4 + tt
                b = tt % 2
                O("sync", lambda s, _gt=gt, _b=b: s.dma_start(
                    yb[_b][:], y_s[ts(_gt), :]),
                    reads=[f"ybr{b}"], writes=[f"yb{b}"], dma=True, tag="y_ld")
                O("vector", lambda v, _tt=tt, _gt=gt: v.scalar_tensor_tensor(
                    ep[:], outacc[:, _tt * 2048:(_tt + 1) * 2048],
                    sh_cols[:, _gt:_gt + 1], bo2bc[:], ALU.mult, ALU.add),
                    reads=["outacc", "sh_cols", "bo2bc"], writes=["ep"], tag="ep_sc")
                O("vector", lambda v, _b=b: v.tensor_tensor(
                    ep[:], ep[:], yb[_b][:], ALU.add),
                    reads=["ep", f"yb{b}"], writes=["ep", f"ybr{b}"], tag="ep_res")
                O("scalar", lambda s, _tt=tt: s.activation(
                    outacc[:, _tt * 2048:(_tt + 1) * 2048], ep[:],
                    AF.Square, bias=0.0, scale=1.0, accum_out=stat2[:, 0:1]),
                    reads=["ep"], writes=["outacc", "s2t0"], tag="l2_sq")
                O("scalar", lambda s, _tt=tt: s.activation(
                    outacc[:, _tt * 2048:(_tt + 1) * 2048], ep[:],
                    AF.Copy, bias=0.0, scale=1.0 / D, accum_out=stat2[:, 2:3]),
                    reads=["ep"], writes=["outacc", "s2t2"], tag="l2_mean")
                O("vector", lambda v: v.tensor_scalar(
                    stat2[:, 3:4], stat2[:, 0:1], 1.0 / D, None, ALU.mult),
                    reads=["s2t0"], writes=["s2t3"], tag="l2_ex2")
                O("scalar", lambda s: s.activation(
                    stat2[:, 4:5], stat2[:, 2:3], AF.Square, bias=0.0, scale=1.0),
                    reads=["s2t2"], writes=["s2t4"], tag="l2_m2")
                O("vector", lambda v: v.tensor_scalar(
                    stat2[:, 14:15], stat2[:, 3:4], stat2[:, 4:5],
                    EPS, ALU.subtract, ALU.add),
                    reads=["s2t3", "s2t4"], writes=["s2t14"], tag="l2_var")
                O("scalar", lambda s: s.activation(
                    stat2[:, 6:7], stat2[:, 14:15], AF.Ln, bias=0.0, scale=1.0),
                    reads=["s2t14"], writes=["s2t6"], tag="l2_ln")
                O("scalar", lambda s: s.activation(
                    stat2[:, 7:8], stat2[:, 6:7], AF.Exp, bias=0.0, scale=-0.5),
                    reads=["s2t6"], writes=["s2t7"], tag="l2_rstd")
                O("vector", lambda v: v.tensor_tensor(
                    stat2[:, 8:9], stat2[:, 2:3], stat2[:, 7:8], ALU.mult),
                    reads=["s2t2", "s2t7"], writes=["s2t8"], tag="l2_mr")
                O("vector", lambda v: v.tensor_scalar(
                    stat2[:, 9:10], stat2[:, 8:9], -1.0, None, ALU.mult),
                    reads=["s2t8"], writes=["s2t9"], tag="l2_nmr")
                O("vector", lambda v, _b=b: v.tensor_scalar(
                    yb[_b][:], ep[:], stat2[:, 7:8], stat2[:, 9:10],
                    ALU.mult, ALU.add),
                    reads=["ep", "s2t9", "s2t7"], writes=[f"yb{b}"], tag="l2_y")
                O("sync", lambda s, _gt=gt, _b=b: s.dma_start(
                    out[ts(_gt), :], yb[_b][:]),
                    reads=[f"yb{b}"], writes=["out", f"ybr{b}"], dma=True,
                    tag="out_st")

            p5_loads(0, 0)
            p5_quant(0, 0)
            for th in range(4):
                for hc in range(8):
                    b = hc % 2
                    if hc + 1 < 8:
                        p5_loads(th, hc + 1)
                        p5_quant(th, hc + 1)
                        if hc == 6 and th + 1 < 4:
                            p5_loads_h(th + 1, 0)
                    elif th + 1 < 4:
                        p5_loads_w(th + 1, 0)
                    for tt in range(4):
                        if hc == 0 and th >= 1:
                            ep_tt(th - 1, tt)
                        for ec in range(4):
                            bi = (tt * 4 + ec) % 5
                            bank = x_ps[bi]
                            bk = f"x_ps{bi}"
                            for hsub in range(8):
                                O("tensor", lambda t, _tt=tt, _ec=ec, _hs=hsub, _b=bank, _bb=b: t.matmul(
                                    _b[:],
                                    nh_sb[_bb][:, _hs * 512 + _tt * 128:_hs * 512 + (_tt + 1) * 128],
                                    wo2t[_bb][:, _hs * 2048 + _ec * 512:_hs * 2048 + (_ec + 1) * 512],
                                    start=(_hs == 0), stop=(_hs == 7)),
                                    reads=[f"nh_sb{b}", f"wo2t{b}"], writes=[bk],
                                    tag="o_mm")
                            if hc == 0:
                                ceng = "scalar" if th >= 1 else "vector"
                                O(ceng, lambda v, _tt=tt, _ec=ec, _b=bank: (
                                    v.tensor_copy(
                                        outacc[:, _tt * 2048 + _ec * 512:_tt * 2048 + (_ec + 1) * 512],
                                        _b[:])
                                    if hasattr(v, "tensor_copy") else
                                    v.activation(
                                        outacc[:, _tt * 2048 + _ec * 512:_tt * 2048 + (_ec + 1) * 512],
                                        _b[:], AF.Copy, bias=0.0, scale=1.0)),
                                    reads=[bk], writes=["outacc"], tag="o_cp")
                            else:
                                O("vector", lambda v, _tt=tt, _ec=ec, _b=bank: v.tensor_tensor(
                                    outacc[:, _tt * 2048 + _ec * 512:_tt * 2048 + (_ec + 1) * 512],
                                    outacc[:, _tt * 2048 + _ec * 512:_tt * 2048 + (_ec + 1) * 512],
                                    _b[:], ALU.add),
                                    reads=[bk, "outacc"], writes=["outacc"], tag="o_add")
                    if hc == 7 and th + 1 < 4:
                        p5_quant(th + 1, 0)
                    O("tensor", lambda t: t.nop(), reads=[f"wo2t{b}", f"nh_sb{b}"],
                      writes=[f"wo2r{b}"], tag="o_fence")
            for tt in range(4):
                ep_tt(3, tt)
            _maybe_taps(nc, p, taps, None)
            p.flush()
    return nc, stack


def _maybe_taps(nc, p, taps, phase):
    if phase is not None and STOP_AFTER != phase:
        return
    for name in DEBUG_TAPS:
        apx = taps[name]
        dst = nc.dram_tensor(f"tap_{name}", list(apx.shape), apx.dtype,
                             kind="ExternalOutput").ap()
        p.op("gpsimd", lambda g, _d=dst, _s=apx: g.dma_start(_d, _s),
             reads=[name], dma=True, tag=f"tap_{name}")


def core_in_map(core, x_full, w_qkv, w_out, b_out, gate_w, gate_b, val_w,
                val_b, out_w, out_b):
    B, hh = core // 2, core % 2
    ac = np.ascontiguousarray
    wq = np.concatenate([
        w_qkv[:, 1024 * hh:1024 * (hh + 1)],
        w_qkv[:, 2048 + 1024 * hh:2048 + 1024 * (hh + 1)],
        w_qkv[:, 4096 + 1024 * hh:4096 + 1024 * (hh + 1)]], axis=1)
    return dict(
        x=ac(x_full[B]),
        xres=ac(x_full[B, MYT * hh:MYT * (hh + 1)]),
        wqkv=ac(wq),
        wout=ac(w_out[1024 * hh:1024 * (hh + 1)]),
        bout=ac(b_out),
        wg=ac(gate_w.T),
        wv=ac(val_w.T),
        wo2=ac(out_w.T),
        bg=ac(gate_b),
        bv=ac(val_b),
        bo2=ac(out_b),
    )


def kernel(L0_attn_x, L0_attn_w_qkv, L0_attn_w_out, L0_attn_b_out,
           L0_ffn_gate_w, L0_ffn_gate_b, L0_ffn_val_w, L0_ffn_val_b,
           L0_ffn_out_w, L0_ffn_out_b):
    x = np.asarray(L0_attn_x, dtype=np.float32)
    args = [np.asarray(a, dtype=np.float32) for a in (
        L0_attn_w_qkv, L0_attn_w_out, L0_attn_b_out, L0_ffn_gate_w,
        L0_ffn_gate_b, L0_ffn_val_w, L0_ffn_val_b, L0_ffn_out_w, L0_ffn_out_b)]
    in_maps = [core_in_map(c, x, *args) for c in range(8)]
    nc, stack = build_nc()
    try:
        res = run_bass_kernel_spmd(nc, in_maps, core_ids=list(range(8)))
    finally:
        stack.close()
    out = np.empty((4, T, D), dtype=np.float32)
    for c in range(8):
        B, hh = c // 2, c % 2
        out[B, MYT * hh:MYT * (hh + 1), :] = np.asarray(res.results[c]["out"])
    return out
